# version 47
# baseline (speedup 1.0000x reference)
import contextlib
import numpy as np
import ml_dtypes
import concourse.bass as bass
import concourse.mybir as mybir
from concourse.bass_utils import run_bass_kernel_spmd

F32 = mybir.dt.float32
BF16 = mybir.dt.bfloat16
AF = mybir.ActivationFunctionType
ALU = mybir.AluOpType

C = 1024
T = 4096
TOWN = 2048
NCT = 8
NP = 8
NTC_OWN = 4
NTC_ALL = 8
VST = 1040
NKT = 32


def build_nc():
    nc = bass.Bass("TRN2", target_bir_lowering=False)

    x_t = nc.dram_tensor("x_t", [C, T], F32, kind="ExternalInput")
    x_o = nc.dram_tensor("x_o", [C, TOWN], F32, kind="ExternalInput")
    wq_t = nc.dram_tensor("wq_t", [C, C], F32, kind="ExternalInput")
    wk_t = nc.dram_tensor("wk_t", [C, C], F32, kind="ExternalInput")
    wv_t = nc.dram_tensor("wv_t", [C, C], F32, kind="ExternalInput")
    wo_t = nc.dram_tensor("wo_t", [C, C], F32, kind="ExternalInput")
    wf_t = nc.dram_tensor("wf_t", [C, C], F32, kind="ExternalInput")
    vr_i = nc.dram_tensor("vr_i", [C, 32], F32, kind="ExternalInput")
    ct_i = nc.dram_tensor("ct_i", [32, C], F32, kind="ExternalInput")
    bq_i = nc.dram_tensor("bq_i", [128, 8], F32, kind="ExternalInput")
    bk_i = nc.dram_tensor("bk_i", [128, 8], F32, kind="ExternalInput")
    bo_i = nc.dram_tensor("bo_i", [128, 8], F32, kind="ExternalInput")
    bf_i = nc.dram_tensor("bf_i", [128, 8], F32, kind="ExternalInput")
    bv_i = nc.dram_tensor("bv_i", [128, 8], F32, kind="ExternalInput")
    msk_i = nc.dram_tensor("msk_i", [8, 128, 512], BF16, kind="ExternalInput")
    cst_i = nc.dram_tensor("cst_i", [128, 384], BF16, kind="ExternalInput")
    cv_i = nc.dram_tensor("cv_i", [128, 8], F32, kind="ExternalInput")
    out_t = nc.dram_tensor("out_t", [C, TOWN], F32, kind="ExternalOutput")
    q_dram = nc.dram_tensor("q_dram", [C, TOWN], BF16)
    g_dram = nc.dram_tensor("g_dram", [C, TOWN], F32)
    v_dram = nc.dram_tensor("v_dram", [NP, NKT, 128, 130], BF16)
    nrm_scr = nc.dram_tensor("nrm_scr", [2, 1024], F32)
    lnr_f = nc.dram_tensor("lnr_f", [12, 512], F32)
    lnr_b = nc.dram_tensor("lnr_b", [12, 1536], BF16)
    corr_scr = nc.dram_tensor("corr_scr", [8, 1024], BF16)

    stck = contextlib.ExitStack()
    with stck:
        ent = stck.enter_context
        k_sb = ent(nc.sbuf_tensor("k_sb", [128, NP * T], BF16))
        vp_sb = ent(nc.sbuf_tensor("vp_sb", [128, 2 * NKT * 130], BF16))
        vp_f = vp_sb.bitcast(F32)
        q_sb = ent(nc.sbuf_tensor("q_sb", [128, 2 * TOWN], BF16))
        w_sb = ent(nc.sbuf_tensor("w_sb", [128, 16384], BF16))
        vr_sb = ent(nc.sbuf_tensor("vr_sb", [128, 256], BF16))
        ct_sb = ent(nc.sbuf_tensor("ct_sb", [32, 1024], BF16))
        ar1 = ent(nc.sbuf_tensor("ar1", [128, 20480], BF16))
        ar1f = ar1.bitcast(F32)
        ar2 = ent(nc.sbuf_tensor("ar2", [128, 4096], F32))
        ar2b = ar2.bitcast(BF16)
        stg = ent(nc.sbuf_tensor("stg", [128, 1024], F32))
        cst_sb = ent(nc.sbuf_tensor("cst_sb", [128, 384], BF16))
        cv_sb = ent(nc.sbuf_tensor("cv_sb", [128, 8], F32))
        bq_sb = ent(nc.sbuf_tensor("bq_sb", [128, 8], F32))
        bk_sb = ent(nc.sbuf_tensor("bk_sb", [128, 8], F32))
        bo_sb = ent(nc.sbuf_tensor("bo_sb", [128, 8], F32))
        bf_sb = ent(nc.sbuf_tensor("bf_sb", [128, 8], F32))
        bv_sb = ent(nc.sbuf_tensor("bv_sb", [128, 8], F32))
        p_sb = ent(nc.sbuf_tensor("p_sb", [128, 4 * 512], BF16))
        corr_sb = ent(nc.sbuf_tensor("corr_sb", [2, 4096], BF16))
        xr_sb = ent(nc.sbuf_tensor("xr_sb", [32, 512], BF16))
        gs_sb = ent(nc.sbuf_tensor("gs_sb", [128, 512], F32))
        op_sb = ent(nc.sbuf_tensor("op_sb", [128, 2 * 512], F32))
        op_bf = op_sb.bitcast(BF16)
        xo_sb = ent(nc.sbuf_tensor("xo_sb", [128, 512], F32))
        gp_sb = ent(nc.sbuf_tensor("gp_sb", [128, 512], F32))

        psS = [ent(nc.psum_tensor(f"psS{i}", [128, 1024], F32)) for i in range(2)]
        psO = [ent(nc.psum_tensor(f"psO{i}", [128, 512], F32)) for i in range(2)]
        psB = ent(nc.psum_tensor("psB", [128, 512], F32))
        psX = ent(nc.psum_tensor("psX", [128, 512], F32))
        class PSV:
            def __init__(self, h, co):
                self.h, self.co = h, co

            def __getitem__(self, idx):
                r, c = idx
                cs = 0 if c.start is None else c.start
                ce = 512 if c.stop is None else c.stop
                return self.h[r, self.co + cs: self.co + ce]

        ps = [PSV(psS[0], 0), PSV(psS[0], 512), PSV(psS[1], 0),
              PSV(psS[1], 512), PSV(psO[0], 0), PSV(psO[1], 0),
              PSV(psB, 0), PSV(psX, 0)]

        def xn_ap(buf, c):
            o = buf * 4096 + c * 512
            return ar1[:, o:o + 512]

        def xs_ap(sl, sub):
            o = 4608 + sl * 2048 + sub * 512
            return ar1f[:, o:o + 512]

        def xsq_ap(s):
            o = 8192 + s * 512
            return ar1[:, o:o + 512]

        def qev_ap(s):
            if s < 2:
                return ar1[:, 17408 + s * 512: 17408 + s * 512 + 512]
            return ar1[:, 18432:18944]

        def gpv_ap(s):
            if s < 2:
                return ar1f[:, 9216 + s * 512: 9216 + s * 512 + 512]
            return ar1f[:, 8704:9216]

        def o_half_ap(p, m, half):
            o = p * 2048 + m * 512
            return ar1[64 * half:64 * half + 64, o:o + 512]

        def o_rhs_ap(cp, tc):
            o = cp * 2048 + tc * 512
            return ar1[:, o:o + 512]

        def msk_ap(i):
            o = 16384 + i * 512
            return ar1[:, o:o + 512]

        bc_l = ar2[:, 1024:1536]
        bc_l_lo = ar2[0:64, 1024:1536]
        bc_l_hi = ar2[64:128, 1024:1536]
        r_m = ar2[0:1, 1536:2048]
        r_e = ar2[0:1, 2048:2560]
        r_t = ar2[0:1, 2560:3072]

        for i, v in enumerate([-4.0, 0.125, 1e-5, -0.5, -1.0, 1.0, 0.5]):
            nc.const_aps.register(F32, v, cv_sb[:, i:i + 1])

        DMA_SEMS = ["dsy", "dgp", "dsc", "aux", "msk", "x0", "x1",
                    "stg0", "stg1", "q0", "q1", "v0", "v1", "gp0", "gp1",
                    "xoA", "xoB", "qe0", "qe1", "ge0", "ge1", "ve0", "ve1",
                    "op0", "op1", "nb0", "nb1", "nc0", "nc1",
                    "lb0", "lb1", "lc0", "lc1",
                    "cs0", "cbq", "cbk", "cbv", "cbg", "qe2", "ge2",
                    "xs0", "xs1", "xs2", "xs3",
                    "xs4", "xs5", "xs6", "xs7",
                    "xt0", "xt1", "xt2", "xt3",
                    "xt4", "xt5", "xt6", "xt7", "lm0", "lm1"]
        sems = {s: ent(nc.semaphore(s)) for s in
                ["pe", "act", "dve", "pool"] + DMA_SEMS}

        def plan(role, eng):
            cnt = {k: 0 for k in
                   ["pe", "act", "dve", "pool"] + DMA_SEMS}
            lastw = {}

            def step(engine, waits, inc, builder):
                for s, v in waits:
                    assert v <= cnt[s], f"wait {s}>={v} emitted={cnt[s]}"
                amt = 16 if inc in DMA_SEMS else 1
                cnt[inc] += amt
                if engine != role:
                    return
                for s, v in waits:
                    if v > 0 and lastw.get(s, 0) < v:
                        eng.wait_ge(sems[s], v)
                        lastw[s] = v
                builder(eng).then_inc(sems[inc], amt)

            def mm(waits, out, lhsT, rhs, start, stop):
                step("tensor", waits, "pe",
                     lambda e: e.matmul(out, lhsT, rhs, start=start, stop=stop,
                                        skip_group_check=True))

            for dst, src in [(cst_sb[:, :], cst_i[:, :]),
                             (cv_sb[:, :], cv_i[:, :]),
                             (bq_sb[:, :], bq_i[:, :]),
                             (bk_sb[:, :], bk_i[:, :]),
                             (bo_sb[:, :], bo_i[:, :]),
                             (bf_sb[:, :], bf_i[:, :]),
                             (bv_sb[:, :], bv_i[:, :])]:
                step("sync", [], "aux",
                     lambda e, d=dst, s=src: e.dma_start(out=d, in_=s))
            c0 = cnt["aux"]

            stage_war = [0, 0]
            stage_idx = [0]

            def stg_ap(s):
                return stg[:, s * 512:(s + 1) * 512]

            wload = {}

            def stg_half(wdram, base, c, half):
                s = stage_idx[0] % 2
                stage_idx[0] += 1
                sm = "stg0" if s == 0 else "stg1"
                w = [("pool", stage_war[s])] if stage_war[s] else []
                step("scalar", w, sm,
                     lambda e, c=c, s=s, half=half, wd=wdram: e.dma_start(
                         out=stg_ap(s),
                         in_=wd[c * 128:(c + 1) * 128,
                                half * 512: half * 512 + 512]))
                dmark = cnt[sm]
                step("gpsimd", [(sm, dmark)], "pool",
                     lambda e, c=c, s=s, half=half, b=base: e.tensor_copy(
                         w_sb[:, b + c * 1024 + half * 512:
                              b + c * 1024 + half * 512 + 512],
                         stg_ap(s)))
                stage_war[s] = cnt["pool"]

            def w_unit(wdram, base, c, tag):
                stg_half(wdram, base, c, 0)
                stg_half(wdram, base, c, 1)
                wload[(tag, c)] = cnt["pool"]

            chains = [(x_o, TOWN, t) for t in range(4)] + \
                     [(x_t, T, t) for t in range(8)]
            lbm = {}
            lcm = {}
            lc_mark = [0, 0]
            row_pe_war = [0, 0]
            row_dve_war = [0, 0]
            xslot_war = {(p, c): 0 for p in range(2) for c in range(8)}
            xsq_war = [0, 0]
            stats_war = 0
            xn_war = [0, 0]
            xrdy = {}
            cast_mark = {}
            statstop = {}
            mc_war = [None]
            cs_cast = [0]
            corr_mark = {}

            MC = ar2b[0:1, 2048:3072]

            def MB(s):
                return ar2b[0:2, 6144 + s * 512: 6656 + s * 512]

            def MBr(s):
                return ar2b[0:1, 7168 + s * 512: 7680 + s * 512]

            RST = p_sb[0:1, 1024:1536]

            def bcr_ap(s):
                return ar2[:, s * 512:(s + 1) * 512]

            xseq = [0]
            xpar = {}

            def x_src(p2, c):
                if p2 == 0:
                    return bass.AP(ar1f, 4608 + c * 512,
                                   [[10240, 128], [1, 512]])
                return bass.AP(vp_f, c * 512, [[4160, 128], [1, 512]])

            def x_dma(i, key):
                xd, co, tc = chains[i]
                p2 = xseq[0] % 2
                xseq[0] += 1
                xpar[key] = p2
                for c in range(NCT):
                    sm = ("xs%d" if p2 == 0 else "xt%d") % c
                    w = []
                    if xslot_war[(p2, c)]:
                        w = [("dve", xslot_war[(p2, c)])]

                    def bld(e, c=c, xd=xd, co=co, tc=tc, p2=p2):
                        dap = bass.AP(xd, c * 128 * co + tc * 512,
                                      [[co, 128], [1, 512]])
                        return e.dma_start(out=x_src(p2, c), in_=dap)
                    step("sync", w, sm, bld)
                    xrdy[(key, c)] = (sm, cnt[sm])

            def xs_sub(sl, sub):
                o = 4608 + sl * 2048 + sub * 512
                return bass.AP(ar1f, o, [[10240, 128], [1, 512]])

            def casts(i, key):
                buf = i % 2
                p2 = xpar[key]
                for c in range(NCT):
                    wv = [xrdy[(key, c)]]
                    if c == 0 and xn_war[buf]:
                        wv.append(("pe", xn_war[buf]))
                    step("vector", wv, "dve",
                         lambda e, buf=buf, c=c, p2=p2: e.tensor_copy(
                             xn_ap(buf, c), x_src(p2, c)))
                    cast_mark[(i, c)] = cnt["dve"]
                    xslot_war[(p2, c)] = cnt["dve"]

            def stat_pair(i, c):
                buf = i % 2
                sq = c % 2
                wv = [("dve", cast_mark[(i, c)])]
                if xsq_war[sq]:
                    wv.append(("pe", xsq_war[sq]))
                step("vector", wv, "dve",
                     lambda e, buf=buf, c=c, sq=sq: e.tensor_mul(
                         xsq_ap(sq), xn_ap(buf, c), xn_ap(buf, c)))
                vmark = cnt["dve"]
                w0 = [("dve", vmark)]
                if c == 0 and stats_war:
                    w0.append(("dve", stats_war))
                mm(w0, ps[0][0:1, :], cst_sb[:, 0:1], xn_ap(buf, c),
                   c == 0, c == 7)
                mm([], ps[1][0:1, :], cst_sb[:, 0:1], xsq_ap(sq),
                   c == 0, c == 7)
                xsq_war[sq] = cnt["pe"]
                if c == 7:
                    statstop[i] = cnt["pe"]

            lmm = {}
            lcb = {}

            def back_mu(i):
                s = i % 2
                sM = "lm0" if s == 0 else "lm1"
                wc = [lbm[i]]
                if row_pe_war[s]:
                    wc.append(("pe", row_pe_war[s]))
                step("sync", wc, sM,
                     lambda e, s=s, i=i: e.dma_start(
                         out=MB(s),
                         in_=bass.AP(lnr_b, i * 1536, [[512, 2], [1, 512]])))
                step("sync", [], sM,
                     lambda e, s=s, i=i: e.dma_start(
                         out=MBr(s),
                         in_=bass.AP(lnr_b, i * 1536 + 1024,
                                     [[1, 1], [1, 512]])))
                lmm[i] = cnt[sM]

            def back_bc(i):
                s = i % 2
                sC = "lc0" if s == 0 else "lc1"
                wc = [lbm[i]]
                if row_dve_war[s]:
                    wc.append(("dve", row_dve_war[s]))
                step("sync", wc, sC,
                     lambda e, s=s, i=i: e.dma_start(
                         out=bcr_ap(s),
                         in_=bass.AP(lnr_f, i * 512, [[0, 128], [1, 512]])))
                lcb[i] = cnt[sC]

            def back3(i):
                back_mu(i)
                back_bc(i)

            rowsA = {}

            def ln_rows_a(i):
                nonlocal stats_war
                w = [("pe", statstop[i])]
                if i >= 1 and (i - 1) in lbm:
                    w.append(lbm[i - 1])
                if i >= 1 and (i - 1) in rowsA:
                    w.append(("act", rowsA[i - 1]))
                step("vector", w, "dve",
                     lambda e: e.tensor_scalar(out=r_m, in0=ps[0][0:1, :],
                                               scalar1=1.0 / 1024.0,
                                               scalar2=None, op0=ALU.mult))
                step("vector", [], "dve",
                     lambda e: e.tensor_scalar(out=r_e, in0=ps[1][0:1, :],
                                               scalar1=1.0 / 1024.0,
                                               scalar2=None, op0=ALU.mult))
                stats_war = cnt["dve"]
                step("vector", [], "dve", lambda e: e.tensor_mul(r_t, r_m, r_m))
                step("vector", [], "dve", lambda e: e.tensor_sub(r_e, r_e, r_t))
                wmc = [mc_war[0]] if mc_war[0] else []
                step("vector", wmc, "dve",
                     lambda e: e.tensor_copy(MC[0:1, 0:512], r_m))
                vmark = cnt["dve"]
                step("scalar", [("dve", vmark), ("aux", c0)], "act",
                     lambda e: e.activation(r_t, r_e, AF.Ln, bias=1e-5))
                step("scalar", [], "act",
                     lambda e: e.activation(r_e, r_t, AF.Exp, scale=-0.5))
                step("scalar", [], "act",
                     lambda e: e.activation(r_m, r_t, AF.Exp, scale=0.5))
                rowsA[i] = cnt["act"]

            def ln_rows_b(i):
                s = i % 2
                sA = "lb0" if s == 0 else "lb1"
                amark = rowsA[i]
                step("vector", [("act", amark)], "dve",
                     lambda e: e.tensor_copy(MC[0:1, 512:1024], r_m))
                step("vector", [], "dve",
                     lambda e: e.tensor_copy(RST, r_e))
                ivmark = cnt["dve"]
                step("sync", [("dve", ivmark)], sA,
                     lambda e, i=i: e.dma_start(
                         out=bass.AP(lnr_b, i * 1536, [[1, 1], [1, 1024]]),
                         in_=MC))
                step("sync", [], sA,
                     lambda e, i=i: e.dma_start(
                         out=bass.AP(lnr_b, i * 1536 + 1024,
                                     [[1, 1], [1, 512]]),
                         in_=RST))
                step("sync", [], sA,
                     lambda e, i=i: e.dma_start(out=lnr_f[i:i + 1, :], in_=r_e))
                lbm[i] = (sA, cnt[sA])
                mc_war[0] = lbm[i]
                back_mu(i)
                back_bc(i)

            def ln_rows(i):
                ln_rows_a(i)
                ln_rows_b(i)

            def colsum_1024(pi, wbase, wtag, sem, extra=None, extra_w=None):
                for half in range(2):
                    for c in range(NCT):
                        w = [("pool", wload[(wtag, c)])]
                        if c == 0:
                            if cs_cast[0]:
                                w.append(("dve", cs_cast[0]))
                        mm(w, psX[0:1, 0:512], cst_sb[:, 128:129],
                           w_sb[:, wbase + c * 1024 + half * 512:
                                wbase + c * 1024 + half * 512 + 512],
                           c == 0, c == 7 and extra is None)
                    if extra is not None:
                        mm(extra_w, psX[0:1, 0:512], extra,
                           ct_sb[0:32, half * 512: half * 512 + 512],
                           False, True)
                    pemark = cnt["pe"]
                    wv = [("pe", pemark)]
                    if mc_war[0]:
                        wv.append(mc_war[0])
                    step("vector", wv, "dve",
                         lambda e, half=half: e.tensor_copy(
                             MC[0:1, half * 512:(half + 1) * 512],
                             psX[0:1, 0:512]))
                    cs_cast[0] = cnt["dve"]
                vmark = cnt["dve"]
                step("sync", [("dve", vmark)], "cs0",
                     lambda e, pi=pi: e.dma_start(
                         out=corr_scr[2 * pi:2 * pi + 1, :], in_=MC))
                csm = cnt["cs0"]
                mc_war[0] = ("cs0", csm)
                step("sync", [("cs0", csm)], sem,
                     lambda e, pi=pi: e.dma_start(
                         out=corr_sb[0:2, pi * 1024:(pi + 1) * 1024],
                         in_=corr_scr[2 * pi:2 * pi + 2, :]))
                corr_mark[pi] = (sem, cnt[sem])

            projp_war = {4: 0, 5: 0, 6: 0}
            qev_war = [0, 0, 0]
            gpv_war = [0, 0, 0]
            vev_war = [0, 0]
            xr_war = 0
            ps6_war = 0
            vcolB_war = [0]

            def rank1_ok(i):
                return ("lm0" if i % 2 == 0 else "lm1", lmm[i])

            def evac_ok(i):
                return ("lc0" if i % 2 == 0 else "lc1", lcb[i])

            def casts_two(i, key, c0):
                buf = i % 2
                p2 = xpar[key]
                for c in (c0, c0 + 1):
                    wv = [xrdy[(key, c)]]
                    if c == 0 and xn_war[buf]:
                        wv.append(("pe", xn_war[buf]))
                    step("vector", wv, "dve",
                         lambda e, buf=buf, c=c, p2=p2: e.tensor_copy(
                             xn_ap(buf, c), x_src(p2, c)))
                    cast_mark[(i, c)] = cnt["dve"]
                    xslot_war[(p2, c)] = cnt["dve"]

            def hooks(ft, nxt_cast, nxt_stats):
                if nxt_cast is not None and ft < 2:
                    i2, key = nxt_cast
                    casts_two(i2, key, 4 * ft)
                    casts_two(i2, key, 4 * ft + 2)
                if nxt_stats is not None:
                    if ft == 0:
                        stat_pair(nxt_stats, 0)
                        stat_pair(nxt_stats, 1)
                    if ft == 1:
                        stat_pair(nxt_stats, 2)
                        stat_pair(nxt_stats, 3)
                    if ft == 2:
                        for c in range(4, 8):
                            stat_pair(nxt_stats, c)
                        ln_rows_a(nxt_stats)
                    if ft == 3:
                        ln_rows_b(nxt_stats)

            def qproj(i, nxt_cast, nxt_stats):
                buf = i % 2
                s = i % 2
                tc = i
                for ft in range(NP):
                    bk = 4 + ft % 3
                    for c in range(NCT):
                        w = []
                        if c == 0:
                            w = [("dve", cast_mark[(i, 7)])]
                            if projp_war[bk]:
                                w.append(("dve", projp_war[bk]))
                            if ft == 0:
                                w.append(corr_mark[0])
                                w.append(rank1_ok(i))
                        mm(w, ps[bk][:, :],
                           w_sb[:, c * 1024 + ft * 128:
                                c * 1024 + ft * 128 + 128],
                           xn_ap(buf, c), c == 0, False)
                    mm([], ps[bk][:, :],
                       corr_sb[0:2, ft * 128: ft * 128 + 128],
                       MB(s), False, True)
                    pemark = cnt["pe"]
                    hooks(ft, nxt_cast, nxt_stats)
                    pp = ft % 3
                    qsm = ["qe0", "qe1", "qe2"][pp]
                    wv = [("pe", pemark)]
                    if ft == 0:
                        wv.append(evac_ok(i))
                    if qev_war[pp]:
                        wv.append((qsm, qev_war[pp]))
                    step("vector", wv, "dve",
                         lambda e, bk=bk, pp=pp, s=s: e.tensor_tensor(
                             qev_ap(pp), ps[bk][:, :], bcr_ap(s),
                             ALU.mult))
                    projp_war[bk] = cnt["dve"]
                    vmark = cnt["dve"]
                    step("gpsimd", [("dve", vmark)], qsm,
                         lambda e, ft=ft, tc=tc, pp=pp: e.dma_start(
                             out=q_dram[ft * 128:(ft + 1) * 128,
                                        tc * 512:(tc + 1) * 512],
                             in_=qev_ap(pp)))
                    qev_war[pp] = cnt[qsm]
                xn_war[buf] = cnt["pe"]
                row_pe_war[s] = cnt["pe"]
                row_dve_war[s] = cnt["dve"]

            def gproj(i, nxt_cast, nxt_stats, bk_next=None, precast=None):
                nonlocal xr_war, ps6_war
                buf = i % 2
                s = i % 2
                tc = i
                for c in range(NCT):
                    w = []
                    if c == 0:
                        w = [("dve", cast_mark[(i, 7)]), ("pool", vr_done)]
                        if xr_war:
                            w.append(("pe", xr_war))
                        if ps6_war:
                            w.append(("dve", ps6_war))
                        if projp_war[6]:
                            w.append(("dve", projp_war[6]))
                    mm(w, ps[6][0:32, :], vr_sb[:, c * 32:(c + 1) * 32],
                       xn_ap(buf, c), c == 0, c == 7)
                pemark = cnt["pe"]
                step("vector", [("pe", pemark)], "dve",
                     lambda e: e.tensor_copy(xr_sb[0:32, :], ps[6][0:32, :]))
                ps6_war = cnt["dve"]
                projp_war[6] = 0
                xrmark = cnt["dve"]
                if precast is not None:
                    casts(*precast)
                for ft in range(NP):
                    bk = 4 + ft % 2
                    pp = ft % 2
                    for c in range(NCT):
                        w = []
                        if c == 0:
                            w = [("dve", xrmark)]
                            if projp_war[bk]:
                                w.append(("dve", projp_war[bk]))
                            if ft == 0:
                                w.append(corr_mark[3])
                                w.append(rank1_ok(i))
                                w.append(("pool", wload[("wf", 7)]))
                                w.append(("pool", ct_done))
                        mm(w, ps[bk][:, :],
                           w_sb[:, 8192 + c * 1024 + ft * 128:
                                8192 + c * 1024 + ft * 128 + 128],
                           xn_ap(buf, c), c == 0, False)
                    mm([], ps[bk][:, :],
                       ct_sb[0:32, ft * 128:(ft + 1) * 128], xr_sb[0:32, :],
                       False, False)
                    xr_war = cnt["pe"]
                    mm([], ps[bk][:, :],
                       corr_sb[0:2, 3072 + ft * 128: 3072 + ft * 128 + 128],
                       MB(s), False, True)
                    pemark = cnt["pe"]
                    hooks(ft, nxt_cast, nxt_stats)
                    pp3 = ft % 3
                    gsm = ["ge0", "ge1", "ge2"][pp3]
                    wv = [("pe", pemark)]
                    if ft == 0:
                        wv.append(evac_ok(i))
                    if gpv_war[pp3]:
                        wv.append((gsm, gpv_war[pp3]))
                    else:
                        for qs in ("qe0", "qe1", "qe2"):
                            if cnt[qs]:
                                wv.append((qs, cnt[qs]))
                    step("vector", wv, "dve",
                         lambda e, bk=bk, pp3=pp3, s=s: e.tensor_tensor(
                             gpv_ap(pp3), ps[bk][:, :], bcr_ap(s),
                             ALU.mult))
                    projp_war[bk] = cnt["dve"]
                    vmark = cnt["dve"]
                    step("gpsimd", [("dve", vmark)], gsm,
                         lambda e, ft=ft, tc=tc, pp3=pp3: e.dma_start(
                             out=g_dram[ft * 128:(ft + 1) * 128,
                                        tc * 512:(tc + 1) * 512],
                             in_=gpv_ap(pp3)))
                    gpv_war[pp3] = cnt[gsm]
                    if ft == 4 and bk_next is not None:
                        back3(bk_next)
                xn_war[buf] = cnt["pe"]
                row_pe_war[s] = max(row_pe_war[s], cnt["pe"])
                row_dve_war[s] = max(row_dve_war[s], cnt["dve"])

            def kproj(i, nxt_cast, nxt_stats):
                buf = i % 2
                s = i % 2
                tc = i - 4
                for ft in range(NP):
                    bk = 4 + ft % 3
                    for c in range(NCT):
                        w = []
                        if c == 0:
                            w = [("dve", cast_mark[(i, 7)])]
                            if projp_war[bk]:
                                w.append(("dve", projp_war[bk]))
                            if ft == 0:
                                w.append(corr_mark[1])
                                w.append(rank1_ok(i))
                                w.append(("pool", wload[("wk", 7)]))
                        mm(w, ps[bk][:, :],
                           w_sb[:, c * 1024 + ft * 128:
                                c * 1024 + ft * 128 + 128],
                           xn_ap(buf, c), c == 0, False)
                    mm([], ps[bk][:, :],
                       corr_sb[0:2, 1024 + ft * 128: 1024 + ft * 128 + 128],
                       MB(s), False, True)
                    pemark = cnt["pe"]
                    hooks(ft, nxt_cast, nxt_stats)
                    wv = [("pe", pemark)]
                    if ft == 0:
                        wv.append(evac_ok(i))
                    step("vector", wv, "dve",
                         lambda e, ft=ft, tc=tc, bk=bk, s=s: e.tensor_tensor(
                             k_sb[:, ft * T + tc * 512:
                                  ft * T + tc * 512 + 512],
                             ps[bk][:, :], bcr_ap(s), ALU.mult))
                    projp_war[bk] = cnt["dve"]
                xn_war[buf] = cnt["pe"]
                row_pe_war[s] = max(row_pe_war[s], cnt["pe"])
                row_dve_war[s] = max(row_dve_war[s], cnt["dve"])

            def vproj(i, nxt_cast, bk_next=None):
                buf = i % 2
                s = i % 2
                tc = i - 4
                for tt4 in range(4):
                    w = []
                    if tt4 == 0:
                        w = [rank1_ok(i)]
                        if vcolB_war[0]:
                            w.append(("dve", vcolB_war[0]))
                        if projp_war[6]:
                            w.append(("dve", projp_war[6]))
                    mm(w, psB[:, tt4:tt4 + 1],
                       MBr(s)[0:1, tt4 * 128:(tt4 + 1) * 128],
                       cst_sb[0:1, 0:1], True, True)
                pemark = cnt["pe"]
                step("vector", [("pe", pemark)], "dve",
                     lambda e, s=s: e.tensor_copy(gs_sb[:, s * 4: s * 4 + 4],
                                                  psB[:, 0:4]))
                vcolB_war[0] = cnt["dve"]
                projp_war[6] = 0
                for tt4 in range(4):
                    tt = tc * 4 + tt4
                    vs = tt % 2
                    for half in range(2):
                        bk = 4 + half
                        for c in range(NCT):
                            w = []
                            if c == 0:
                                if projp_war[bk]:
                                    w = [("dve", projp_war[bk])]
                                if tt4 == 0 and half == 0:
                                    w.append(("dve", cast_mark[(i, 7)]))
                                    w.append(corr_mark[2])
                                    w.append(("pool", wload[("wv", 7)]))
                            mm(w, ps[bk][:, :],
                               xn_ap(buf, c)[:, tt4 * 128:(tt4 + 1) * 128],
                               w_sb[:, 8192 + c * 1024 + half * 512:
                                    8192 + c * 1024 + half * 512 + 512],
                               c == 0, False)
                        mm([], ps[bk][:, :],
                           MB(s)[0:2, tt4 * 128:(tt4 + 1) * 128],
                           corr_sb[0:2, 2048 + half * 512:
                                   2048 + half * 512 + 512],
                           False, True)
                        pemark = cnt["pe"]
                        wv = [("pe", pemark)]
                        if half == 0 and vev_war[vs]:
                            wv.append(("ve0" if vs == 0 else "ve1",
                                       vev_war[vs]))

                        def vevac(e, vs=vs, half=half, bk=bk, s=s, tt4=tt4):
                            pin = bass.AP(psO[bk - 4], 0,
                                          [[512, 128], [64, 8], [1, 64]])
                            vout = bass.AP(q_sb, vs * 1040 + half * 520,
                                           [[4096, 128], [65, 8], [1, 64]])
                            return e.tensor_scalar(
                                out=vout, in0=pin,
                                scalar1=gs_sb[:, s * 4 + tt4:
                                              s * 4 + tt4 + 1],
                                scalar2=None, op0=ALU.mult)
                        gi = 2 * tt4 + half
                        if nxt_cast is not None and gi < 4:
                            i2, key = nxt_cast
                            casts_two(i2, key, 2 * gi)
                        step("vector", wv, "dve", vevac)
                        projp_war[bk] = cnt["dve"]

                    def vones(e, vs=vs):
                        ap = bass.AP(q_sb, vs * 1040 + 64,
                                     [[4096, 128], [65, 16]])
                        return e.memset(ap, 1.0)
                    step("vector", [], "dve", vones)
                    vmark = cnt["dve"]

                    def vst(e, tt=tt, vs=vs):
                        sap = bass.AP(q_sb, vs * 1040,
                                      [[4096, 128], [130, 8], [1, 130]])
                        dap = bass.AP(v_dram, tt * 128 * 130,
                                      [[130, 128], [NKT * 128 * 130, 8],
                                       [1, 130]])
                        return e.dma_start(out=dap, in_=sap)
                    vsm = "ve0" if vs == 0 else "ve1"
                    step("gpsimd", [("dve", vmark)], vsm, vst)
                    vev_war[vs] = cnt[vsm]
                    if tt4 == 0 and bk_next is not None:
                        back_mu(bk_next)
                xn_war[buf] = cnt["pe"]
                row_pe_war[s] = max(row_pe_war[s], cnt["pe"])
                row_dve_war[s] = max(row_dve_war[s], cnt["dve"])

            x_dma(0, ("a", 0))
            for c in range(NCT):
                w_unit(wq_t, 0, c, "wq")
            for c in range(NCT):
                s = c % 2
                sm = "stg0" if s == 0 else "stg1"
                w = [("pool", stage_war[s])] if stage_war[s] else []
                step("scalar", w, sm,
                     lambda e, c=c, s=s: e.dma_start(
                         out=stg_ap(s)[:, 0:32],
                         in_=vr_i[c * 128:(c + 1) * 128, :]))
                dmark = cnt[sm]
                step("gpsimd", [(sm, dmark)], "pool",
                     lambda e, c=c, s=s: e.tensor_copy(
                         vr_sb[:, c * 32:(c + 1) * 32], stg_ap(s)[:, 0:32]))
                stage_war[s] = cnt["pool"]
            vr_done = cnt["pool"]
            step("scalar", [("pool", stage_war[0])], "stg0",
                 lambda e: e.dma_start(out=stg_ap(0)[0:32, :],
                                       in_=ct_i[:, 0:512]))
            step("scalar", [("pool", stage_war[1])], "stg1",
                 lambda e: e.dma_start(out=stg_ap(1)[0:32, :],
                                       in_=ct_i[:, 512:1024]))
            m0, m1 = cnt["stg0"], cnt["stg1"]
            step("gpsimd", [("stg0", m0)], "pool",
                 lambda e: e.tensor_copy(ct_sb[0:32, 0:512],
                                         stg_ap(0)[0:32, :]))
            step("gpsimd", [("stg1", m1)], "pool",
                 lambda e: e.tensor_copy(ct_sb[0:32, 512:1024],
                                         stg_ap(1)[0:32, :]))
            stage_war = [cnt["pool"], cnt["pool"]]
            ct_done = cnt["pool"]

            step("vector", [("aux", c0)], "dve",
                 lambda e: e.tensor_copy(p_sb[:, 0:8], bq_sb[:, :]))
            step("vector", [], "dve",
                 lambda e: e.tensor_copy(p_sb[:, 8:16], bk_sb[:, :]))
            step("vector", [], "dve",
                 lambda e: e.tensor_copy(p_sb[:, 16:24], bv_sb[:, :]))
            step("vector", [], "dve",
                 lambda e: e.tensor_copy(p_sb[:, 24:32], bf_sb[:, :]))
            bcast_mark = cnt["dve"]

            def bias_row(pi):
                for half in range(2):
                    for f4 in range(4):
                        ft = half * 4 + f4
                        w = []
                        if f4 == 0:
                            w = [("dve", bcast_mark)]
                            if cs_cast[0]:
                                w.append(("dve", cs_cast[0]))
                        mm(w, psX[0:1, f4 * 128:(f4 + 1) * 128],
                           p_sb[:, pi * 8 + ft: pi * 8 + ft + 1],
                           cst_sb[:, 256:384], True, True)
                    pemark = cnt["pe"]
                    wv = [("pe", pemark)]
                    if mc_war[0]:
                        wv.append(mc_war[0])
                    step("vector", wv, "dve",
                         lambda e, half=half: e.tensor_copy(
                             MC[0:1, half * 512:(half + 1) * 512],
                             psX[0:1, 0:512]))
                    cs_cast[0] = cnt["dve"]
                vmark = cnt["dve"]
                step("sync", [("dve", vmark)], "cs0",
                     lambda e, pi=pi: e.dma_start(
                         out=corr_scr[2 * pi + 1:2 * pi + 2, :], in_=MC))
                mc_war[0] = ("cs0", cnt["cs0"])

            for pi in range(4):
                bias_row(pi)

            casts(0, ("a", 0))
            for c in range(NCT):
                stat_pair(0, c)
            ln_rows(0)
            x_dma(1, ("a", 1))
            colsum_1024(0, 0, "wq", "cbq")
            for c in range(NCT):
                w = []
                if c == 0:
                    w = [("pool", vr_done), ("dve", cs_cast[0])]
                mm(w, psX[0:32, 0:1], vr_sb[:, c * 32:(c + 1) * 32],
                   cst_sb[:, 128:129], c == 0, c == 7)
            pemark = cnt["pe"]
            step("vector", [("pe", pemark)], "dve",
                 lambda e: e.tensor_copy(p_sb[0:32, 32:33], psX[0:32, 0:1]))
            cs_cast[0] = cnt["dve"]
            vrc_mark = cnt["dve"]

            for i in range(4):
                nc_ = (i + 1, ("a", i + 1)) if i + 1 <= 3 else None
                ns_ = i + 1 if i + 1 <= 3 else None
                qproj(i, nc_, ns_)
                for u in range(2):
                    w_unit(wf_t, 8192, 2 * i + u, "wf")
                if i + 2 <= 3:
                    x_dma(i + 2, ("a", i + 2))

            colsum_1024(3, 8192, "wf", "cbg", extra=p_sb[0:32, 32:33],
                        extra_w=[("dve", vrc_mark), ("pool", ct_done)])
            back3(2)
            x_dma(0, ("g", 0))
            x_dma(1, ("g", 1))
            gproj(2, None, None, bk_next=3)
            w_unit(wk_t, 0, 0, "wk")
            w_unit(wk_t, 0, 1, "wk")
            gproj(3, None, None, bk_next=0, precast=(0, ("g", 0)))
            x_dma(4, ("a", 4))
            w_unit(wk_t, 0, 2, "wk")
            w_unit(wk_t, 0, 3, "wk")
            gproj(0, None, None, bk_next=1, precast=(1, ("g", 1)))
            w_unit(wk_t, 0, 4, "wk")
            w_unit(wk_t, 0, 5, "wk")
            gproj(1, None, 4, precast=(4, ("a", 4)))
            w_unit(wk_t, 0, 6, "wk")
            w_unit(wk_t, 0, 7, "wk")
            p1_spill = [("qe0", cnt["qe0"]), ("qe1", cnt["qe1"]),
                        ("qe2", cnt["qe2"]), ("ge0", cnt["ge0"]),
                        ("ge1", cnt["ge1"]), ("ge2", cnt["ge2"])]

            colsum_1024(1, 0, "wk", "cbk")
            for mi in range(2, 8):
                w = (p1_spill + [("dve", cnt["dve"])]) if mi == 2 else []
                step("sync", w, "msk",
                     lambda e, mi=mi: e.dma_start(out=msk_ap(mi),
                                                  in_=msk_i[mi, :, :]))
            x_dma(5, ("a", 5))
            for i in range(4, 12):
                if i + 2 <= 11:
                    x_dma(i + 2, ("a", i + 2))
                nc_ = (i + 1, ("a", i + 1)) if i + 1 <= 11 else None
                ns_ = i + 1 if i + 1 <= 11 else None
                kproj(i, nc_, ns_)
                if i < 8:
                    for u in range(2):
                        w_unit(wv_t, 8192, 2 * (i - 4) + u, "wv")
                if i == 9:
                    colsum_1024(2, 8192, "wv", "cbv")

            back_mu(10)
            Vorder = [10, 11, 4, 5, 6, 7, 8, 9]
            x_dma(4, ("v", 4))
            for j, tcv in enumerate(Vorder):
                if 3 <= j + 2 < 8:
                    x_dma(Vorder[j + 2], ("v", Vorder[j + 2]))
                nc_ = (Vorder[j + 1], ("v", Vorder[j + 1])) \
                    if 2 <= j + 1 < 8 else None
                vproj(tcv, nc_,
                      bk_next=(Vorder[j + 1] if j + 1 < 8 else None))
                if tcv == 7:
                    for mi in range(2):
                        w = [("dve", cast_mark[(8, 7)])] if mi == 0 else []
                        step("sync", w, "msk",
                             lambda e, mi=mi: e.dma_start(
                                 out=msk_ap(mi), in_=msk_i[mi, :, :]))
                w_unit(wo_t, 0, j, "wo")
            p3_pe = cnt["pe"]
            p3_dve = cnt["dve"]
            p3_spill = p1_spill + [("ve0", cnt["ve0"]), ("ve1", cnt["ve1"])]
            wo_mark = wload[("wo", 7)]
            lb_last = lbm[11]

            gp_war = [0, 0]
            xo_war = [0, 0]
            op_war = [0, 0]
            gs_war = [0, 0]
            gmarks = {}
            xmarks = {}
            tiles = [(tc, ft) for tc in range(NTC_OWN) for ft in range(NP)]

            def xo_ap(sl):
                if sl == 0:
                    return xo_sb[:, 0:512]
                return ar2[:, 3072:3584]

            def gp_ap(sl):
                if sl == 0:
                    return gp_sb[:, 0:512]
                return ar2[:, 0:512]

            def gs_ap(sl):
                if sl == 0:
                    return gs_sb[:, :]
                return ar2[:, 3584:4096]

            def p5_dma(idx):
                tc, ft = tiles[idx]
                pp = idx % 2
                gsm = "gp0" if pp == 0 else "gp1"
                xsm = "xoA" if pp == 0 else "xoB"
                w = list(p1_spill)
                if gp_war[pp]:
                    w.append(("act", gp_war[pp]))
                elif pp == 1:
                    w.append(("dve", p3_dve))
                step("sync", w, gsm,
                     lambda e, ft=ft, tc=tc, pp=pp: e.dma_start(
                         out=gp_ap(pp),
                         in_=g_dram[ft * 128:(ft + 1) * 128,
                                    tc * 512:(tc + 1) * 512]))
                gmarks[idx] = (gsm, cnt[gsm])
                wx = [("dve", xo_war[pp])] if xo_war[pp] else []
                step("sync", wx, xsm,
                     lambda e, ft=ft, tc=tc, pp=pp: e.dma_start(
                         out=xo_ap(pp),
                         in_=x_o[ft * 128:(ft + 1) * 128,
                                 tc * 512:(tc + 1) * 512]))
                xmarks[idx] = (xsm, cnt[xsm])

            mskmark = cnt["msk"]

            qld_war = [0, 0]
            vld_war = [0, 0]
            qrdy = [0, 0]
            vrdy = [0, 0]

            def qvload(p):
                s = p % 2
                w = (list(p3_spill) + [("dve", p3_dve)]) if p <= 1 else (
                    [("pe", qld_war[s])] if qld_war[s] else [])
                qsm = "q0" if s == 0 else "q1"
                step("sync", w, qsm,
                     lambda e, p=p, s=s: e.dma_start(
                         out=q_sb[:, s * 2048: s * 2048 + 2048],
                         in_=q_dram[p * 128:(p + 1) * 128, :]))
                qrdy[s] = cnt[qsm]
                w = (list(p3_spill) + [("dve", p3_dve)]) if p <= 1 else (
                    [("pe", vld_war[s])] if vld_war[s] else [])

                def vload(e, p=p, s=s):
                    sap = bass.AP(vp_sb, s * NKT * 130,
                                  [[2 * NKT * 130, 128], [130, NKT], [1, 130]])
                    dap = bass.AP(v_dram, p * NKT * 128 * 130,
                                  [[130, 128], [128 * 130, NKT], [1, 130]])
                    return e.dma_start(out=sap, in_=dap)
                vsm = "v0" if s == 0 else "v1"
                step("sync", w, vsm, vload)
                vrdy[s] = cnt[vsm]

            sc_free = [0, 0]
            pslot_free = [0, 0, 0]
            onorm_war = [0, 0]
            pending_norm = []
            pend_av = []
            pending_muls = []
            nrm_idx = [0]
            last_muls = [0]
            scrA = [0, 0]
            scrB = [0, 0]
            opr = [[psO[0], psO[1]], [psB, psX]]

            def do_norm(p, m, opair, avdone):
                pn = nrm_idx[0] % 2
                first = nrm_idx[0] == 0
                nrm_idx[0] += 1
                sA = "nb0" if pn == 0 else "nb1"
                sB = "nc0" if pn == 0 else "nc1"
                w4 = [("pe", avdone)]
                if scrA[1 - pn]:
                    w4.append(("nb1" if pn == 0 else "nb0", scrA[1 - pn]))
                if first:
                    w4.append(("dve", p3_dve))
                    w4.append(lb_last)
                step("scalar", w4, "act",
                     lambda e, op0=opair[0]: e.activation(
                         r_m, op0[64:65, 0:512], AF.Ln))
                step("scalar", [], "act",
                     lambda e: e.activation(r_e, r_m, AF.Exp, scale=-1.0))
                step("scalar", [], "act",
                     lambda e, op1=opair[1]: e.activation(
                         r_m, op1[64:65, 0:512], AF.Ln))
                step("scalar", [], "act",
                     lambda e: e.activation(r_t, r_m, AF.Exp, scale=-1.0))
                amark = cnt["act"]
                wb = [("act", amark)]
                if scrB[pn]:
                    wb.append((sB, scrB[pn]))
                step("sync", wb, sA,
                     lambda e, pn=pn: e.dma_start(
                         out=nrm_scr[pn:pn + 1, :], in_=ar2[0:1, 2048:3072]))
                scrA[pn] = cnt[sA]
                wc = [(sA, scrA[pn])]
                if last_muls[0]:
                    wc.append(("dve", last_muls[0]))
                elif first:
                    wc.append(("dve", p3_dve))
                    wc.append(lb_last)
                step("sync", wc, sB,
                     lambda e, pn=pn: e.dma_start(
                         out=ar2[0:64, 1024:1536],
                         in_=bass.AP(nrm_scr, pn * 1024, [[0, 64], [1, 512]])))
                step("sync", [], sB,
                     lambda e, pn=pn: e.dma_start(
                         out=ar2[64:128, 1024:1536],
                         in_=bass.AP(nrm_scr, pn * 1024 + 512,
                                     [[0, 64], [1, 512]])))
                scrB[pn] = cnt[sB]
                bmark = scrB[pn]

                def norm_muls(p=p, m=m, opair=opair, sB=sB, bmark=bmark):
                    step("vector", [(sB, bmark)], "dve",
                         lambda e, p=p, m=m, op0=opair[0]: e.tensor_mul(
                             o_half_ap(p, m, 0), op0[0:64, 0:512], bc_l_lo))
                    step("vector", [], "dve",
                         lambda e, p=p, m=m, op1=opair[1]: e.tensor_mul(
                             o_half_ap(p, m, 1), op1[0:64, 0:512], bc_l_hi))
                    last_muls[0] = cnt["dve"]
                    onorm_war[m % 2] = cnt["dve"]
                pending_muls.append(norm_muls)

            qvload(0)
            for p in range(NP):
                s = p % 2
                for m in range(NTC_OWN):
                    L = 8 * m + 8
                    opair = opr[m % 2]
                    pd = {}

                    def psl(r):
                        if r < 2:
                            return p_sb[:, r * 1024: r * 1024 + 1024]
                        return op_bf[:, 0:1024]

                    for kt in range(L + 1):
                        if kt == 4 and pending_norm:
                            do_norm(*pending_norm.pop())
                        if kt == min(12, L) and pending_muls:
                            pending_muls.pop()()
                        if kt < L:
                            rot = kt % 2
                            r3 = kt % 3
                            for h in range(2):
                                w = []
                                if h == 0:
                                    if kt == 0:
                                        w.append(("q0" if s == 0 else "q1",
                                                  qrdy[s]))
                                        if p == 0 and m == 0:
                                            w.append(("dve", p3_dve))
                                    if sc_free[rot]:
                                        w.append(("act", sc_free[rot]))
                                mm(w, ps[2 * rot + h][:, :],
                                   k_sb[64 * h:64 * h + 64,
                                        p * T + kt * 128: p * T + kt * 128 + 128],
                                   q_sb[64 * h:64 * h + 64,
                                        s * 2048 + m * 512:
                                        s * 2048 + m * 512 + 512],
                                   True, True)
                            if kt == 0 and pend_av:
                                pend_av.pop()()
                            if m == 0 and kt == 2 and p + 1 < NP:
                                qvload(p + 1)
                            if p == 7 and m == 0 and kt == 3:
                                p5_dma(0)
                                p5_dma(1)
                            smark = cnt["pe"]
                            wv = [("pe", smark)]
                            if pslot_free[r3]:
                                wv.append(("pe", pslot_free[r3]))
                            if p == 0 and m == 0 and kt == 0:
                                wv.append(lb_last)
                            step("scalar", wv, "act",
                                 lambda e, r3=r3, rot=rot: e.activation(
                                     psl(r3), psS[rot][:, :], AF.Exp,
                                     bias=-4.0, scale=0.125))
                            sc_free[rot] = cnt["act"]
                            dep = ("act", cnt["act"])
                            if kt >= 8 * m:
                                i = kt - 8 * m
                                for h in range(2):
                                    step("vector",
                                         [dep, ("msk", mskmark)], "dve",
                                         lambda e, r3=r3, h=h, i=i:
                                         e.tensor_mul(
                                             psl(r3)[:, h * 512:
                                                     h * 512 + 512],
                                             psl(r3)[:, h * 512:
                                                     h * 512 + 512],
                                             msk_ap(i)))
                                dep = ("dve", cnt["dve"])
                            pd[kt] = dep
                        if 2 <= kt and kt - 2 <= L - 2:
                            ka = kt - 2
                            rka = ka % 3
                            dep = pd.pop(ka)
                            for h in range(2):
                                w2 = [dep] if h == 0 else []
                                if ka == 0:
                                    if onorm_war[m % 2]:
                                        w2.append(("dve", onorm_war[m % 2]))
                                    if h == 0:
                                        w2.append(("v0" if s == 0 else "v1",
                                                   vrdy[s]))
                                mm(w2, opair[h][0:65, 0:512],
                                   vp_sb[:, s * NKT * 130 + ka * 130 + h * 65:
                                         s * NKT * 130 + ka * 130 + h * 65 + 65],
                                   psl(rka)[:, h * 512: h * 512 + 512],
                                   ka == 0, False)
                            pslot_free[rka] = cnt["pe"]
                        if kt == L:
                            dep = pd.pop(L - 1)

                            def av_tail(p=p, m=m, s=s, L=L, opair=opair,
                                        dep=dep, psl=psl):
                                rka = (L - 1) % 3
                                for h in range(2):
                                    w2 = [dep] if h == 0 else []
                                    mm(w2, opair[h][0:65, 0:512],
                                       vp_sb[:, s * NKT * 130 + (L - 1) * 130
                                             + h * 65:
                                             s * NKT * 130 + (L - 1) * 130
                                             + h * 65 + 65],
                                       psl(rka)[:, h * 512: h * 512 + 512],
                                       False, True)
                                pslot_free[rka] = cnt["pe"]
                                qld_war[s] = cnt["pe"]
                                vld_war[s] = cnt["pe"]
                                pending_norm.append((p, m, opair, cnt["pe"]))
                            pend_av.append(av_tail)
                qld_war[s] = cnt["pe"]
                vld_war[s] = cnt["pe"]
            if pend_av:
                pend_av.pop()()
            if pending_norm:
                do_norm(*pending_norm.pop())
            while pending_muls:
                pending_muls.pop(0)()
            p4_pe = cnt["pe"]

            projp_war = {4: 0, 5: 0, 6: 0}
            for idx, (tc, ft) in enumerate(tiles):
                pp = idx % 2
                bk5 = 4 + idx % 3
                for cp in range(NCT):
                    w2 = []
                    if cp == 0:
                        if projp_war[bk5]:
                            w2 = [("dve", projp_war[bk5])]
                        else:
                            w2 = [("dve", last_muls[0]),
                                  ("pool", wo_mark)]
                    mm(w2, ps[bk5][:, :],
                       w_sb[:, cp * 1024 + ft * 128:
                            cp * 1024 + ft * 128 + 128],
                       o_rhs_ap(cp, tc), cp == 0, cp == 7)
                pemark = cnt["pe"]
                wa = [gmarks[idx]]
                if gs_war[pp]:
                    wa.append(("dve", gs_war[pp]))
                step("scalar", wa, "act",
                     lambda e, pp=pp: e.activation(
                         gs_ap(pp), gp_ap(pp), AF.Silu))
                gp_war[pp] = cnt["act"]
                amark = cnt["act"]
                wv = [("pe", pemark), ("act", amark), xmarks[idx]]
                if op_war[pp]:
                    wv.append(("op0" if pp == 0 else "op1", op_war[pp]))
                step("vector", wv, "dve",
                     lambda e, ft=ft, bk5=bk5: e.tensor_scalar(
                         out=bc_l, in0=ps[bk5][:, :],
                         scalar1=bo_sb[:, ft:ft + 1], scalar2=None,
                         op0=ALU.add))
                projp_war[bk5] = cnt["dve"]
                step("vector", [], "dve",
                     lambda e, pp=pp: e.tensor_mul(bc_l, bc_l, gs_ap(pp)))
                gs_war[pp] = cnt["dve"]
                step("vector", [], "dve",
                     lambda e, pp=pp: e.tensor_add(
                         op_sb[:, pp * 512: pp * 512 + 512], bc_l,
                         xo_ap(pp)))
                xo_war[pp] = cnt["dve"]
                vmark = cnt["dve"]
                osm = "op0" if pp == 0 else "op1"
                step("gpsimd", [("dve", vmark)], osm,
                     lambda e, ft=ft, tc=tc, pp=pp: e.dma_start(
                         out=out_t[ft * 128:(ft + 1) * 128,
                                   tc * 512:(tc + 1) * 512],
                         in_=op_sb[:, pp * 512: pp * 512 + 512]))
                op_war[pp] = cnt[osm]
                if idx + 2 < len(tiles):
                    p5_dma(idx + 2)

            if role == "gpsimd":
                for sname in DMA_SEMS:
                    if cnt[sname]:
                        eng.wait_ge(sems[sname], cnt[sname])

        with nc.Block() as block:
            block.tensor(lambda e: plan("tensor", e))
            block.scalar(lambda e: plan("scalar", e))
            block.vector(lambda e: plan("vector", e))
            block.sync(lambda e: plan("sync", e))
            block.gpsimd(lambda e: plan("gpsimd", e))

    return nc


def make_inputs(x, qkv_w, qkv_b, out_w, out_b, W, b, V_r, Cm):
    bf16 = ml_dtypes.bfloat16
    tri = np.zeros((4, 128, 512), dtype=np.float32)
    for r in range(4):
        kk = np.arange(128)[:, None]
        qq = np.arange(512)[None, :]
        tri[r] = (qq >= kk + 128 * r).astype(np.float32)
    ones_t = np.ones((128, 512), dtype=np.float32)
    zeros_t = np.zeros((128, 512), dtype=np.float32)
    cst = np.concatenate([np.ones((128, 128), np.float32),
                          -np.ones((128, 128), np.float32),
                          np.eye(128, dtype=np.float32)], axis=1)
    cv = np.tile(np.array([[-4.0, 0.125, 1e-5, -0.5, -1.0, 1.0, 0.5, 0.0]],
                          np.float32), (128, 1))
    wq_t = np.ascontiguousarray(qkv_w[0:1024].T)
    wk_t = np.ascontiguousarray(qkv_w[1024:2048].T)
    wv_t = np.ascontiguousarray(qkv_w[2048:3072].T)
    wo_t = np.ascontiguousarray(out_w.T)
    wf_t = np.ascontiguousarray(W.T)
    ct_t = np.ascontiguousarray(Cm.T)
    bq = np.ascontiguousarray(qkv_b[0:1024].reshape(8, 128).T)
    bk = np.ascontiguousarray(qkv_b[1024:2048].reshape(8, 128).T)
    bv = np.ascontiguousarray(qkv_b[2048:3072].reshape(8, 128).T)
    bo = np.ascontiguousarray(out_b.reshape(8, 128).T)
    bf = np.ascontiguousarray(b.reshape(8, 128).T)
    in_maps = []
    for core in range(8):
        bb, j = core // 2, core % 2
        xt = np.ascontiguousarray(x[bb].T)
        own = np.concatenate(
            [xt[:, (2 * m + j) * 512:(2 * m + j + 1) * 512] for m in range(4)],
            axis=1)
        if j == 0:
            msk = np.stack([tri[0], tri[1], tri[2], tri[3],
                            zeros_t, zeros_t, zeros_t, zeros_t])
        else:
            msk = np.stack([ones_t, ones_t, ones_t, ones_t,
                            tri[0], tri[1], tri[2], tri[3]])
        in_maps.append({
            "x_t": xt, "x_o": np.ascontiguousarray(own),
            "wq_t": wq_t, "wk_t": wk_t, "wv_t": wv_t, "wo_t": wo_t,
            "wf_t": wf_t, "vr_i": np.ascontiguousarray(V_r),
            "ct_i": ct_t, "bq_i": bq, "bk_i": bk, "bo_i": bo, "bf_i": bf,
            "bv_i": bv, "msk_i": msk.astype(bf16),
            "cst_i": cst.astype(bf16), "cv_i": cv,
        })
    return in_maps


def kernel(L0_attn_x, L0_attn_qkv_w, L0_attn_qkv_b, L0_attn_out_w,
           L0_attn_out_b, L0_ffn_W, L0_ffn_b, L0_ffn_V_r, L0_ffn_C):
    x = np.asarray(L0_attn_x, dtype=np.float32)
    in_maps = make_inputs(x,
                          np.asarray(L0_attn_qkv_w, np.float32),
                          np.asarray(L0_attn_qkv_b, np.float32),
                          np.asarray(L0_attn_out_w, np.float32),
                          np.asarray(L0_attn_out_b, np.float32),
                          np.asarray(L0_ffn_W, np.float32),
                          np.asarray(L0_ffn_b, np.float32),
                          np.asarray(L0_ffn_V_r, np.float32),
                          np.asarray(L0_ffn_C, np.float32))
    nc = build_nc()
    res = run_bass_kernel_spmd(nc, in_maps, core_ids=list(range(8)))
    out = np.empty((4, 4096, 1024), dtype=np.float32)
    for bb in range(4):
        obT = np.empty((1024, 4096), dtype=np.float32)
        for j in range(2):
            o = np.asarray(res.results[2 * bb + j]["out_t"])
            for m in range(4):
                obT[:, (2 * m + j) * 512:(2 * m + j + 1) * 512] = \
                    o[:, m * 512:(m + 1) * 512]
        out[bb] = obT.T
    return out


if __name__ == "__main__":
    nc = build_nc()
    print("build ok")


# version 48 (final)
# speedup vs baseline: 1.0449x; 1.0449x over previous
import contextlib
import numpy as np
import ml_dtypes
import concourse.bass as bass
import concourse.mybir as mybir
from concourse.bass_utils import run_bass_kernel_spmd

F32 = mybir.dt.float32
BF16 = mybir.dt.bfloat16
AF = mybir.ActivationFunctionType
ALU = mybir.AluOpType

C = 1024
T = 4096
TOWN = 2048
NCT = 8
NP = 8
NTC_OWN = 4
NTC_ALL = 8
VST = 1040
NKT = 32


def build_nc():
    nc = bass.Bass("TRN2", target_bir_lowering=False)

    x_t = nc.dram_tensor("x_t", [C, T], F32, kind="ExternalInput")
    x_o = nc.dram_tensor("x_o", [C, TOWN], F32, kind="ExternalInput")
    wq_t = nc.dram_tensor("wq_t", [C, C], F32, kind="ExternalInput")
    wk_t = nc.dram_tensor("wk_t", [C, C], F32, kind="ExternalInput")
    wv_t = nc.dram_tensor("wv_t", [C, C], F32, kind="ExternalInput")
    wo_t = nc.dram_tensor("wo_t", [C, C], F32, kind="ExternalInput")
    wf_t = nc.dram_tensor("wf_t", [C, C], F32, kind="ExternalInput")
    vr_i = nc.dram_tensor("vr_i", [C, 32], F32, kind="ExternalInput")
    ct_i = nc.dram_tensor("ct_i", [32, C], F32, kind="ExternalInput")
    bq_i = nc.dram_tensor("bq_i", [128, 8], F32, kind="ExternalInput")
    bk_i = nc.dram_tensor("bk_i", [128, 8], F32, kind="ExternalInput")
    bo_i = nc.dram_tensor("bo_i", [128, 8], F32, kind="ExternalInput")
    bf_i = nc.dram_tensor("bf_i", [128, 8], F32, kind="ExternalInput")
    bv_i = nc.dram_tensor("bv_i", [128, 8], F32, kind="ExternalInput")
    msk_i = nc.dram_tensor("msk_i", [8, 128, 512], BF16, kind="ExternalInput")
    cst_i = nc.dram_tensor("cst_i", [128, 384], BF16, kind="ExternalInput")
    cv_i = nc.dram_tensor("cv_i", [128, 8], F32, kind="ExternalInput")
    out_t = nc.dram_tensor("out_t", [C, TOWN], F32, kind="ExternalOutput")
    q_dram = nc.dram_tensor("q_dram", [C, TOWN], BF16)
    g_dram = nc.dram_tensor("g_dram", [C, TOWN], F32)
    v_dram = nc.dram_tensor("v_dram", [NP, NKT, 128, 130], BF16)
    nrm_scr = nc.dram_tensor("nrm_scr", [2, 1024], F32)
    lnr_f = nc.dram_tensor("lnr_f", [12, 512], F32)
    lnr_b = nc.dram_tensor("lnr_b", [12, 1536], BF16)
    corr_scr = nc.dram_tensor("corr_scr", [8, 1024], BF16)

    stck = contextlib.ExitStack()
    with stck:
        ent = stck.enter_context
        k_sb = ent(nc.sbuf_tensor("k_sb", [128, NP * T], BF16))
        vp_sb = ent(nc.sbuf_tensor("vp_sb", [128, 2 * NKT * 130], BF16))
        vp_f = vp_sb.bitcast(F32)
        q_sb = ent(nc.sbuf_tensor("q_sb", [128, 2 * TOWN], BF16))
        w_sb = ent(nc.sbuf_tensor("w_sb", [128, 16384], BF16))
        vr_sb = ent(nc.sbuf_tensor("vr_sb", [128, 256], BF16))
        ct_sb = ent(nc.sbuf_tensor("ct_sb", [32, 1024], BF16))
        ar1 = ent(nc.sbuf_tensor("ar1", [128, 20480], BF16))
        ar1f = ar1.bitcast(F32)
        ar2 = ent(nc.sbuf_tensor("ar2", [128, 4096], F32))
        ar2b = ar2.bitcast(BF16)
        stg = ent(nc.sbuf_tensor("stg", [128, 1024], F32))
        cst_sb = ent(nc.sbuf_tensor("cst_sb", [128, 384], BF16))
        cv_sb = ent(nc.sbuf_tensor("cv_sb", [128, 8], F32))
        bq_sb = ent(nc.sbuf_tensor("bq_sb", [128, 8], F32))
        bk_sb = ent(nc.sbuf_tensor("bk_sb", [128, 8], F32))
        bo_sb = ent(nc.sbuf_tensor("bo_sb", [128, 8], F32))
        bf_sb = ent(nc.sbuf_tensor("bf_sb", [128, 8], F32))
        bv_sb = ent(nc.sbuf_tensor("bv_sb", [128, 8], F32))
        p_sb = ent(nc.sbuf_tensor("p_sb", [128, 4 * 512], BF16))
        corr_sb = ent(nc.sbuf_tensor("corr_sb", [2, 4096], BF16))
        xr_sb = ent(nc.sbuf_tensor("xr_sb", [32, 512], BF16))
        gs_sb = ent(nc.sbuf_tensor("gs_sb", [128, 512], F32))
        op_sb = ent(nc.sbuf_tensor("op_sb", [128, 2 * 512], F32))
        op_bf = op_sb.bitcast(BF16)
        xo_sb = ent(nc.sbuf_tensor("xo_sb", [128, 512], F32))
        gp_sb = ent(nc.sbuf_tensor("gp_sb", [128, 512], F32))

        psS = [ent(nc.psum_tensor(f"psS{i}", [128, 1024], F32)) for i in range(2)]
        psO = [ent(nc.psum_tensor(f"psO{i}", [128, 512], F32)) for i in range(2)]
        psB = ent(nc.psum_tensor("psB", [128, 512], F32))
        psX = ent(nc.psum_tensor("psX", [128, 512], F32))
        class PSV:
            def __init__(self, h, co):
                self.h, self.co = h, co

            def __getitem__(self, idx):
                r, c = idx
                cs = 0 if c.start is None else c.start
                ce = 512 if c.stop is None else c.stop
                return self.h[r, self.co + cs: self.co + ce]

        ps = [PSV(psS[0], 0), PSV(psS[0], 512), PSV(psS[1], 0),
              PSV(psS[1], 512), PSV(psO[0], 0), PSV(psO[1], 0),
              PSV(psB, 0), PSV(psX, 0)]

        def xn_ap(buf, c):
            o = buf * 4096 + c * 512
            return ar1[:, o:o + 512]

        def xs_ap(sl, sub):
            o = 4608 + sl * 2048 + sub * 512
            return ar1f[:, o:o + 512]

        def xsq_ap(s):
            o = 8192 + s * 512
            return ar1[:, o:o + 512]

        def qev_ap(s):
            if s < 2:
                return ar1[:, 17408 + s * 512: 17408 + s * 512 + 512]
            return ar1[:, 18432:18944]

        def gpv_ap(s):
            if s < 2:
                return ar1f[:, 9216 + s * 512: 9216 + s * 512 + 512]
            return ar1f[:, 8704:9216]

        def o_half_ap(p, m, half):
            o = p * 2048 + m * 512
            return ar1[64 * half:64 * half + 64, o:o + 512]

        def o_rhs_ap(cp, tc):
            o = cp * 2048 + tc * 512
            return ar1[:, o:o + 512]

        def msk_ap(i):
            o = 16384 + i * 512
            return ar1[:, o:o + 512]

        bc_l = ar2[:, 1024:1536]
        bc_l_lo = ar2[0:64, 1024:1536]
        bc_l_hi = ar2[64:128, 1024:1536]
        r_m = ar2[0:1, 1536:2048]
        r_e = ar2[0:1, 2048:2560]
        r_t = ar2[0:1, 2560:3072]

        for i, v in enumerate([-4.0, 0.125, 1e-5, -0.5, -1.0, 1.0, 0.5]):
            nc.const_aps.register(F32, v, cv_sb[:, i:i + 1])

        DMA_SEMS = ["dsy", "dgp", "dsc", "aux", "msk", "x0", "x1",
                    "stg0", "stg1", "q0", "q1", "v0", "v1", "gp0", "gp1",
                    "xoA", "xoB", "qe0", "qe1", "ge0", "ge1", "ve0", "ve1",
                    "op0", "op1", "nb0", "nb1", "nc0", "nc1",
                    "lb0", "lb1", "lc0", "lc1",
                    "cs0", "cbq", "cbk", "cbv", "cbg", "qe2", "ge2",
                    "xs0", "xs1", "xs2", "xs3",
                    "xs4", "xs5", "xs6", "xs7",
                    "xt0", "xt1", "xt2", "xt3",
                    "xt4", "xt5", "xt6", "xt7", "lm0", "lm1"]
        sems = {s: ent(nc.semaphore(s)) for s in
                ["pe", "act", "dve", "pool"] + DMA_SEMS}

        def plan(role, eng):
            cnt = {k: 0 for k in
                   ["pe", "act", "dve", "pool"] + DMA_SEMS}
            lastw = {}

            def step(engine, waits, inc, builder):
                for s, v in waits:
                    assert v <= cnt[s], f"wait {s}>={v} emitted={cnt[s]}"
                amt = 16 if inc in DMA_SEMS else 1
                cnt[inc] += amt
                if engine != role:
                    return
                for s, v in waits:
                    if v > 0 and lastw.get(s, 0) < v:
                        eng.wait_ge(sems[s], v)
                        lastw[s] = v
                builder(eng).then_inc(sems[inc], amt)

            def mm(waits, out, lhsT, rhs, start, stop):
                step("tensor", waits, "pe",
                     lambda e: e.matmul(out, lhsT, rhs, start=start, stop=stop,
                                        skip_group_check=True))

            for dst, src in [(cst_sb[:, :], cst_i[:, :]),
                             (cv_sb[:, :], cv_i[:, :]),
                             (bq_sb[:, :], bq_i[:, :]),
                             (bk_sb[:, :], bk_i[:, :]),
                             (bo_sb[:, :], bo_i[:, :]),
                             (bf_sb[:, :], bf_i[:, :]),
                             (bv_sb[:, :], bv_i[:, :])]:
                step("sync", [], "aux",
                     lambda e, d=dst, s=src: e.dma_start(out=d, in_=s))
            c0 = cnt["aux"]

            stage_war = [0, 0]
            stage_idx = [0]

            def stg_ap(s):
                return stg[:, s * 512:(s + 1) * 512]

            wload = {}

            def stg_half(wdram, base, c, half):
                s = stage_idx[0] % 2
                stage_idx[0] += 1
                sm = "stg0" if s == 0 else "stg1"
                w = [("pool", stage_war[s])] if stage_war[s] else []
                step("scalar", w, sm,
                     lambda e, c=c, s=s, half=half, wd=wdram: e.dma_start(
                         out=stg_ap(s),
                         in_=wd[c * 128:(c + 1) * 128,
                                half * 512: half * 512 + 512]))
                dmark = cnt[sm]
                step("gpsimd", [(sm, dmark)], "pool",
                     lambda e, c=c, s=s, half=half, b=base: e.tensor_copy(
                         w_sb[:, b + c * 1024 + half * 512:
                              b + c * 1024 + half * 512 + 512],
                         stg_ap(s)))
                stage_war[s] = cnt["pool"]

            def w_unit(wdram, base, c, tag):
                stg_half(wdram, base, c, 0)
                stg_half(wdram, base, c, 1)
                wload[(tag, c)] = cnt["pool"]

            chains = [(x_o, TOWN, t) for t in range(4)] + \
                     [(x_t, T, t) for t in range(8)]
            lbm = {}
            lcm = {}
            lc_mark = [0, 0]
            row_pe_war = [0, 0]
            row_dve_war = [0, 0]
            xslot_war = {(p, c): 0 for p in range(2) for c in range(8)}
            xsq_war = [0, 0]
            stats_war = 0
            xn_war = [0, 0]
            xrdy = {}
            cast_mark = {}
            statstop = {}
            mc_war = [None]
            cs_cast = [0]
            corr_mark = {}

            MC = ar2b[0:1, 2048:3072]

            def MB(s):
                return ar2b[0:2, 6144 + s * 512: 6656 + s * 512]

            def MBr(s):
                return ar2b[0:1, 7168 + s * 512: 7680 + s * 512]

            RST = p_sb[0:1, 1024:1536]

            def bcr_ap(s):
                return ar2[:, s * 512:(s + 1) * 512]

            xseq = [0]
            xpar = {}

            def x_src(p2, c):
                if p2 == 0:
                    return bass.AP(ar1f, 4608 + c * 512,
                                   [[10240, 128], [1, 512]])
                return bass.AP(vp_f, c * 512, [[4160, 128], [1, 512]])

            def x_dma(i, key):
                xd, co, tc = chains[i]
                p2 = xseq[0] % 2
                xseq[0] += 1
                xpar[key] = p2
                for c in range(NCT):
                    sm = ("xs%d" if p2 == 0 else "xt%d") % c
                    w = []
                    if xslot_war[(p2, c)]:
                        w = [xslot_war[(p2, c)]]

                    def bld(e, c=c, xd=xd, co=co, tc=tc, p2=p2):
                        dap = bass.AP(xd, c * 128 * co + tc * 512,
                                      [[co, 128], [1, 512]])
                        return e.dma_start(out=x_src(p2, c), in_=dap)
                    step("sync", w, sm, bld)
                    xrdy[(key, c)] = (sm, cnt[sm])

            def xs_sub(sl, sub):
                o = 4608 + sl * 2048 + sub * 512
                return bass.AP(ar1f, o, [[10240, 128], [1, 512]])

            def cast_one(i, key, c, buf, p2):
                wv = [xrdy[(key, c)]]
                if c <= 1 and xn_war[buf]:
                    wv.append(("pe", xn_war[buf]))
                if c % 2 == 0:
                    step("vector", wv, "dve",
                         lambda e, buf=buf, c=c, p2=p2: e.tensor_copy(
                             xn_ap(buf, c), x_src(p2, c)))
                    cast_mark[(i, c)] = ("dve", cnt["dve"])
                else:
                    step("scalar", wv, "act",
                         lambda e, buf=buf, c=c, p2=p2: e.activation(
                             xn_ap(buf, c), x_src(p2, c), AF.Copy))
                    cast_mark[(i, c)] = ("act", cnt["act"])
                xslot_war[(p2, c)] = cast_mark[(i, c)]

            def casts(i, key):
                buf = i % 2
                p2 = xpar[key]
                for c in range(NCT):
                    cast_one(i, key, c, buf, p2)

            def stat_pair(i, c):
                buf = i % 2
                sq = c % 2
                wv = [cast_mark[(i, c)]]
                if xsq_war[sq]:
                    wv.append(("pe", xsq_war[sq]))
                step("vector", wv, "dve",
                     lambda e, buf=buf, c=c, sq=sq: e.tensor_mul(
                         xsq_ap(sq), xn_ap(buf, c), xn_ap(buf, c)))
                vmark = cnt["dve"]
                w0 = [("dve", vmark)]
                if c == 0 and stats_war:
                    w0.append(("dve", stats_war))
                mm(w0, ps[0][0:1, :], cst_sb[:, 0:1], xn_ap(buf, c),
                   c == 0, c == 7)
                mm([], ps[1][0:1, :], cst_sb[:, 0:1], xsq_ap(sq),
                   c == 0, c == 7)
                xsq_war[sq] = cnt["pe"]
                if c == 7:
                    statstop[i] = cnt["pe"]

            lmm = {}
            lcb = {}

            def back_mu(i):
                s = i % 2
                sM = "lm0" if s == 0 else "lm1"
                wc = [lbm[i]]
                if row_pe_war[s]:
                    wc.append(("pe", row_pe_war[s]))
                step("sync", wc, sM,
                     lambda e, s=s, i=i: e.dma_start(
                         out=MB(s),
                         in_=bass.AP(lnr_b, i * 1536, [[512, 2], [1, 512]])))
                step("sync", [], sM,
                     lambda e, s=s, i=i: e.dma_start(
                         out=MBr(s),
                         in_=bass.AP(lnr_b, i * 1536 + 1024,
                                     [[1, 1], [1, 512]])))
                lmm[i] = cnt[sM]

            def back_bc(i):
                s = i % 2
                sC = "lc0" if s == 0 else "lc1"
                wc = [lbm[i]]
                if row_dve_war[s]:
                    wc.append(("dve", row_dve_war[s]))
                step("sync", wc, sC,
                     lambda e, s=s, i=i: e.dma_start(
                         out=bcr_ap(s),
                         in_=bass.AP(lnr_f, i * 512, [[0, 128], [1, 512]])))
                lcb[i] = cnt[sC]

            def back3(i):
                back_mu(i)
                back_bc(i)

            rowsA = {}

            def ln_rows_a(i):
                nonlocal stats_war
                w = [("pe", statstop[i])]
                if i >= 1 and (i - 1) in lbm:
                    w.append(lbm[i - 1])
                if i >= 1 and (i - 1) in rowsA:
                    w.append(("act", rowsA[i - 1]))
                step("vector", w, "dve",
                     lambda e: e.tensor_scalar(out=r_m, in0=ps[0][0:1, :],
                                               scalar1=1.0 / 1024.0,
                                               scalar2=None, op0=ALU.mult))
                step("vector", [], "dve",
                     lambda e: e.tensor_scalar(out=r_e, in0=ps[1][0:1, :],
                                               scalar1=1.0 / 1024.0,
                                               scalar2=None, op0=ALU.mult))
                stats_war = cnt["dve"]
                step("vector", [], "dve", lambda e: e.tensor_mul(r_t, r_m, r_m))
                step("vector", [], "dve", lambda e: e.tensor_sub(r_e, r_e, r_t))
                wmc = [mc_war[0]] if mc_war[0] else []
                step("vector", wmc, "dve",
                     lambda e: e.tensor_copy(MC[0:1, 0:512], r_m))
                vmark = cnt["dve"]
                step("scalar", [("dve", vmark), ("aux", c0)], "act",
                     lambda e: e.activation(r_t, r_e, AF.Ln, bias=1e-5))
                step("scalar", [], "act",
                     lambda e: e.activation(r_e, r_t, AF.Exp, scale=-0.5))
                step("scalar", [], "act",
                     lambda e: e.activation(r_m, r_t, AF.Exp, scale=0.5))
                rowsA[i] = cnt["act"]

            def ln_rows_b(i):
                s = i % 2
                sA = "lb0" if s == 0 else "lb1"
                amark = rowsA[i]
                step("vector", [("act", amark)], "dve",
                     lambda e: e.tensor_copy(MC[0:1, 512:1024], r_m))
                step("vector", [], "dve",
                     lambda e: e.tensor_copy(RST, r_e))
                ivmark = cnt["dve"]
                step("sync", [("dve", ivmark)], sA,
                     lambda e, i=i: e.dma_start(
                         out=bass.AP(lnr_b, i * 1536, [[1, 1], [1, 1024]]),
                         in_=MC))
                step("sync", [], sA,
                     lambda e, i=i: e.dma_start(
                         out=bass.AP(lnr_b, i * 1536 + 1024,
                                     [[1, 1], [1, 512]]),
                         in_=RST))
                step("sync", [], sA,
                     lambda e, i=i: e.dma_start(out=lnr_f[i:i + 1, :], in_=r_e))
                lbm[i] = (sA, cnt[sA])
                mc_war[0] = lbm[i]
                back_mu(i)
                back_bc(i)

            def ln_rows(i):
                ln_rows_a(i)
                ln_rows_b(i)

            def colsum_1024(pi, wbase, wtag, sem, extra=None, extra_w=None):
                for half in range(2):
                    for c in range(NCT):
                        w = [("pool", wload[(wtag, c)])]
                        if c == 0:
                            if cs_cast[0]:
                                w.append(("dve", cs_cast[0]))
                        mm(w, psX[0:1, 0:512], cst_sb[:, 128:129],
                           w_sb[:, wbase + c * 1024 + half * 512:
                                wbase + c * 1024 + half * 512 + 512],
                           c == 0, c == 7 and extra is None)
                    if extra is not None:
                        mm(extra_w, psX[0:1, 0:512], extra,
                           ct_sb[0:32, half * 512: half * 512 + 512],
                           False, True)
                    pemark = cnt["pe"]
                    wv = [("pe", pemark)]
                    if mc_war[0]:
                        wv.append(mc_war[0])
                    step("vector", wv, "dve",
                         lambda e, half=half: e.tensor_copy(
                             MC[0:1, half * 512:(half + 1) * 512],
                             psX[0:1, 0:512]))
                    cs_cast[0] = cnt["dve"]
                vmark = cnt["dve"]
                step("sync", [("dve", vmark)], "cs0",
                     lambda e, pi=pi: e.dma_start(
                         out=corr_scr[2 * pi:2 * pi + 1, :], in_=MC))
                csm = cnt["cs0"]
                mc_war[0] = ("cs0", csm)
                step("sync", [("cs0", csm)], sem,
                     lambda e, pi=pi: e.dma_start(
                         out=corr_sb[0:2, pi * 1024:(pi + 1) * 1024],
                         in_=corr_scr[2 * pi:2 * pi + 2, :]))
                corr_mark[pi] = (sem, cnt[sem])

            projp_war = {4: 0, 5: 0, 6: 0}
            qev_war = [0, 0, 0]
            gpv_war = [0, 0, 0]
            vev_war = [0, 0]
            xr_war = 0
            ps6_war = 0
            vcolB_war = [0]

            def rank1_ok(i):
                return ("lm0" if i % 2 == 0 else "lm1", lmm[i])

            def evac_ok(i):
                return ("lc0" if i % 2 == 0 else "lc1", lcb[i])

            def casts_two(i, key, c0):
                buf = i % 2
                p2 = xpar[key]
                for c in (c0, c0 + 1):
                    cast_one(i, key, c, buf, p2)

            def hooks(ft, nxt_cast, nxt_stats):
                if nxt_cast is not None and ft < 2:
                    i2, key = nxt_cast
                    casts_two(i2, key, 4 * ft)
                    casts_two(i2, key, 4 * ft + 2)
                if nxt_stats is not None:
                    if ft == 0:
                        stat_pair(nxt_stats, 0)
                        stat_pair(nxt_stats, 1)
                    if ft == 1:
                        stat_pair(nxt_stats, 2)
                        stat_pair(nxt_stats, 3)
                    if ft == 2:
                        for c in range(4, 8):
                            stat_pair(nxt_stats, c)
                        ln_rows_a(nxt_stats)
                    if ft == 3:
                        ln_rows_b(nxt_stats)

            def qproj(i, nxt_cast, nxt_stats):
                buf = i % 2
                s = i % 2
                tc = i
                for ft in range(NP):
                    bk = 4 + ft % 3
                    for c in range(NCT):
                        w = []
                        if ft == 0:
                            w.append(("pool", wload[("wq", c)]))
                        if c == 0:
                            w.append(cast_mark[(i, 7)])
                            w.append(cast_mark[(i, 6)])
                            if projp_war[bk]:
                                w.append(("dve", projp_war[bk]))
                        mm(w, ps[bk][:, :],
                           w_sb[:, c * 1024 + ft * 128:
                                c * 1024 + ft * 128 + 128],
                           xn_ap(buf, c), c == 0, False)
                    w1 = [corr_mark[0], rank1_ok(i)] if ft == 0 else []
                    mm(w1, ps[bk][:, :],
                       corr_sb[0:2, ft * 128: ft * 128 + 128],
                       MB(s), False, True)
                    pemark = cnt["pe"]
                    hooks(ft, nxt_cast, nxt_stats)
                    pp = ft % 3
                    qsm = ["qe0", "qe1", "qe2"][pp]
                    wv = [("pe", pemark)]
                    if ft == 0:
                        wv.append(evac_ok(i))
                    if qev_war[pp]:
                        wv.append((qsm, qev_war[pp]))
                    step("vector", wv, "dve",
                         lambda e, bk=bk, pp=pp, s=s: e.tensor_tensor(
                             qev_ap(pp), ps[bk][:, :], bcr_ap(s),
                             ALU.mult))
                    projp_war[bk] = cnt["dve"]
                    vmark = cnt["dve"]
                    step("gpsimd", [("dve", vmark)], qsm,
                         lambda e, ft=ft, tc=tc, pp=pp: e.dma_start(
                             out=q_dram[ft * 128:(ft + 1) * 128,
                                        tc * 512:(tc + 1) * 512],
                             in_=qev_ap(pp)))
                    qev_war[pp] = cnt[qsm]
                xn_war[buf] = cnt["pe"]
                row_pe_war[s] = cnt["pe"]
                row_dve_war[s] = cnt["dve"]

            def gproj(i, nxt_cast, nxt_stats, bk_next=None, precast=None):
                nonlocal xr_war, ps6_war
                buf = i % 2
                s = i % 2
                tc = i
                for c in range(NCT):
                    w = []
                    if c == 0:
                        w = [cast_mark[(i, 7)], cast_mark[(i, 6)],
                             ("pool", vr_done)]
                        if xr_war:
                            w.append(("pe", xr_war))
                        if ps6_war:
                            w.append(("dve", ps6_war))
                        if projp_war[6]:
                            w.append(("dve", projp_war[6]))
                    mm(w, ps[6][0:32, :], vr_sb[:, c * 32:(c + 1) * 32],
                       xn_ap(buf, c), c == 0, c == 7)
                pemark = cnt["pe"]
                step("vector", [("pe", pemark)], "dve",
                     lambda e: e.tensor_copy(xr_sb[0:32, :], ps[6][0:32, :]))
                ps6_war = cnt["dve"]
                projp_war[6] = 0
                xrmark = cnt["dve"]
                if precast is not None:
                    casts(*precast)
                for ft in range(NP):
                    bk = 4 + ft % 2
                    pp = ft % 2
                    for c in range(NCT):
                        w = []
                        if ft == 0:
                            w.append(("pool", wload[("wf", c)]))
                        if c == 0:
                            w.append(("dve", xrmark))
                            if projp_war[bk]:
                                w.append(("dve", projp_war[bk]))
                        mm(w, ps[bk][:, :],
                           w_sb[:, 8192 + c * 1024 + ft * 128:
                                8192 + c * 1024 + ft * 128 + 128],
                           xn_ap(buf, c), c == 0, False)
                    w1 = [("pool", ct_done)] if ft == 0 else []
                    mm(w1, ps[bk][:, :],
                       ct_sb[0:32, ft * 128:(ft + 1) * 128], xr_sb[0:32, :],
                       False, False)
                    xr_war = cnt["pe"]
                    w1 = [corr_mark[3], rank1_ok(i)] if ft == 0 else []
                    mm(w1, ps[bk][:, :],
                       corr_sb[0:2, 3072 + ft * 128: 3072 + ft * 128 + 128],
                       MB(s), False, True)
                    pemark = cnt["pe"]
                    hooks(ft, nxt_cast, nxt_stats)
                    pp3 = ft % 3
                    gsm = ["ge0", "ge1", "ge2"][pp3]
                    wv = [("pe", pemark)]
                    if ft == 0:
                        wv.append(evac_ok(i))
                    if gpv_war[pp3]:
                        wv.append((gsm, gpv_war[pp3]))
                    else:
                        for qs in ("qe0", "qe1", "qe2"):
                            if cnt[qs]:
                                wv.append((qs, cnt[qs]))
                    step("vector", wv, "dve",
                         lambda e, bk=bk, pp3=pp3, s=s: e.tensor_tensor(
                             gpv_ap(pp3), ps[bk][:, :], bcr_ap(s),
                             ALU.mult))
                    projp_war[bk] = cnt["dve"]
                    vmark = cnt["dve"]
                    step("gpsimd", [("dve", vmark)], gsm,
                         lambda e, ft=ft, tc=tc, pp3=pp3: e.dma_start(
                             out=g_dram[ft * 128:(ft + 1) * 128,
                                        tc * 512:(tc + 1) * 512],
                             in_=gpv_ap(pp3)))
                    gpv_war[pp3] = cnt[gsm]
                    if ft == 4 and bk_next is not None:
                        back3(bk_next)
                xn_war[buf] = cnt["pe"]
                row_pe_war[s] = max(row_pe_war[s], cnt["pe"])
                row_dve_war[s] = max(row_dve_war[s], cnt["dve"])

            def kproj(i, nxt_cast, nxt_stats):
                buf = i % 2
                s = i % 2
                tc = i - 4
                for ft in range(NP):
                    bk = 4 + ft % 3
                    for c in range(NCT):
                        w = []
                        if ft == 0:
                            w.append(("pool", wload[("wk", c)]))
                        if c == 0:
                            w.append(cast_mark[(i, 7)])
                            w.append(cast_mark[(i, 6)])
                            if projp_war[bk]:
                                w.append(("dve", projp_war[bk]))
                        mm(w, ps[bk][:, :],
                           w_sb[:, c * 1024 + ft * 128:
                                c * 1024 + ft * 128 + 128],
                           xn_ap(buf, c), c == 0, False)
                    w1 = [corr_mark[1], rank1_ok(i)] if ft == 0 else []
                    mm(w1, ps[bk][:, :],
                       corr_sb[0:2, 1024 + ft * 128: 1024 + ft * 128 + 128],
                       MB(s), False, True)
                    pemark = cnt["pe"]
                    hooks(ft, nxt_cast, nxt_stats)
                    wv = [("pe", pemark)]
                    if ft == 0:
                        wv.append(evac_ok(i))
                    step("vector", wv, "dve",
                         lambda e, ft=ft, tc=tc, bk=bk, s=s: e.tensor_tensor(
                             k_sb[:, ft * T + tc * 512:
                                  ft * T + tc * 512 + 512],
                             ps[bk][:, :], bcr_ap(s), ALU.mult))
                    projp_war[bk] = cnt["dve"]
                xn_war[buf] = cnt["pe"]
                row_pe_war[s] = max(row_pe_war[s], cnt["pe"])
                row_dve_war[s] = max(row_dve_war[s], cnt["dve"])

            def vproj(i, nxt_cast, bk_next=None):
                buf = i % 2
                s = i % 2
                tc = i - 4
                for tt4 in range(4):
                    w = []
                    if tt4 == 0:
                        w = [rank1_ok(i)]
                        if vcolB_war[0]:
                            w.append(("dve", vcolB_war[0]))
                        if projp_war[6]:
                            w.append(("dve", projp_war[6]))
                    mm(w, psB[:, tt4:tt4 + 1],
                       MBr(s)[0:1, tt4 * 128:(tt4 + 1) * 128],
                       cst_sb[0:1, 0:1], True, True)
                pemark = cnt["pe"]
                step("vector", [("pe", pemark)], "dve",
                     lambda e, s=s: e.tensor_copy(gs_sb[:, s * 4: s * 4 + 4],
                                                  psB[:, 0:4]))
                vcolB_war[0] = cnt["dve"]
                projp_war[6] = 0
                for tt4 in range(4):
                    tt = tc * 4 + tt4
                    vs = tt % 2
                    for half in range(2):
                        bk = 4 + half
                        for c in range(NCT):
                            w = []
                            if tt4 == 0 and half == 0:
                                w.append(("pool", wload[("wv", c)]))
                            if c == 0:
                                if projp_war[bk]:
                                    w.append(("dve", projp_war[bk]))
                                if tt4 == 0 and half == 0:
                                    w.append(cast_mark[(i, 7)])
                                    w.append(cast_mark[(i, 6)])
                            mm(w, ps[bk][:, :],
                               xn_ap(buf, c)[:, tt4 * 128:(tt4 + 1) * 128],
                               w_sb[:, 8192 + c * 1024 + half * 512:
                                    8192 + c * 1024 + half * 512 + 512],
                               c == 0, False)
                        w1 = [corr_mark[2]] if tt4 == 0 and half == 0 else []
                        mm(w1, ps[bk][:, :],
                           MB(s)[0:2, tt4 * 128:(tt4 + 1) * 128],
                           corr_sb[0:2, 2048 + half * 512:
                                   2048 + half * 512 + 512],
                           False, True)
                        pemark = cnt["pe"]
                        wv = [("pe", pemark)]
                        if half == 0 and vev_war[vs]:
                            wv.append(("ve0" if vs == 0 else "ve1",
                                       vev_war[vs]))

                        def vevac(e, vs=vs, half=half, bk=bk, s=s, tt4=tt4):
                            pin = bass.AP(psO[bk - 4], 0,
                                          [[512, 128], [64, 8], [1, 64]])
                            vout = bass.AP(q_sb, vs * 1040 + half * 520,
                                           [[4096, 128], [65, 8], [1, 64]])
                            return e.tensor_scalar(
                                out=vout, in0=pin,
                                scalar1=gs_sb[:, s * 4 + tt4:
                                              s * 4 + tt4 + 1],
                                scalar2=None, op0=ALU.mult)
                        gi = 2 * tt4 + half
                        if nxt_cast is not None and gi < 4:
                            i2, key = nxt_cast
                            casts_two(i2, key, 2 * gi)
                        step("vector", wv, "dve", vevac)
                        projp_war[bk] = cnt["dve"]

                    def vones(e, vs=vs):
                        ap = bass.AP(q_sb, vs * 1040 + 64,
                                     [[4096, 128], [65, 16]])
                        return e.memset(ap, 1.0)
                    step("vector", [], "dve", vones)
                    vmark = cnt["dve"]

                    def vst(e, tt=tt, vs=vs):
                        sap = bass.AP(q_sb, vs * 1040,
                                      [[4096, 128], [130, 8], [1, 130]])
                        dap = bass.AP(v_dram, tt * 128 * 130,
                                      [[130, 128], [NKT * 128 * 130, 8],
                                       [1, 130]])
                        return e.dma_start(out=dap, in_=sap)
                    vsm = "ve0" if vs == 0 else "ve1"
                    step("gpsimd", [("dve", vmark)], vsm, vst)
                    vev_war[vs] = cnt[vsm]
                    if tt4 == 0 and bk_next is not None:
                        back_mu(bk_next)
                xn_war[buf] = cnt["pe"]
                row_pe_war[s] = max(row_pe_war[s], cnt["pe"])
                row_dve_war[s] = max(row_dve_war[s], cnt["dve"])

            x_dma(0, ("a", 0))
            for c in range(NCT):
                w_unit(wq_t, 0, c, "wq")
            for c in range(NCT):
                s = c % 2
                sm = "stg0" if s == 0 else "stg1"
                w = [("pool", stage_war[s])] if stage_war[s] else []
                step("scalar", w, sm,
                     lambda e, c=c, s=s: e.dma_start(
                         out=stg_ap(s)[:, 0:32],
                         in_=vr_i[c * 128:(c + 1) * 128, :]))
                dmark = cnt[sm]
                step("gpsimd", [(sm, dmark)], "pool",
                     lambda e, c=c, s=s: e.tensor_copy(
                         vr_sb[:, c * 32:(c + 1) * 32], stg_ap(s)[:, 0:32]))
                stage_war[s] = cnt["pool"]
            vr_done = cnt["pool"]
            step("scalar", [("pool", stage_war[0])], "stg0",
                 lambda e: e.dma_start(out=stg_ap(0)[0:32, :],
                                       in_=ct_i[:, 0:512]))
            step("scalar", [("pool", stage_war[1])], "stg1",
                 lambda e: e.dma_start(out=stg_ap(1)[0:32, :],
                                       in_=ct_i[:, 512:1024]))
            m0, m1 = cnt["stg0"], cnt["stg1"]
            step("gpsimd", [("stg0", m0)], "pool",
                 lambda e: e.tensor_copy(ct_sb[0:32, 0:512],
                                         stg_ap(0)[0:32, :]))
            step("gpsimd", [("stg1", m1)], "pool",
                 lambda e: e.tensor_copy(ct_sb[0:32, 512:1024],
                                         stg_ap(1)[0:32, :]))
            stage_war = [cnt["pool"], cnt["pool"]]
            ct_done = cnt["pool"]

            step("vector", [("aux", c0)], "dve",
                 lambda e: e.tensor_copy(p_sb[:, 0:8], bq_sb[:, :]))
            step("vector", [], "dve",
                 lambda e: e.tensor_copy(p_sb[:, 8:16], bk_sb[:, :]))
            step("vector", [], "dve",
                 lambda e: e.tensor_copy(p_sb[:, 16:24], bv_sb[:, :]))
            step("vector", [], "dve",
                 lambda e: e.tensor_copy(p_sb[:, 24:32], bf_sb[:, :]))
            bcast_mark = cnt["dve"]

            def bias_row(pi):
                for half in range(2):
                    for f4 in range(4):
                        ft = half * 4 + f4
                        w = []
                        if f4 == 0:
                            w = [("dve", bcast_mark)]
                            if cs_cast[0]:
                                w.append(("dve", cs_cast[0]))
                        mm(w, psX[0:1, f4 * 128:(f4 + 1) * 128],
                           p_sb[:, pi * 8 + ft: pi * 8 + ft + 1],
                           cst_sb[:, 256:384], True, True)
                    pemark = cnt["pe"]
                    wv = [("pe", pemark)]
                    if mc_war[0]:
                        wv.append(mc_war[0])
                    step("vector", wv, "dve",
                         lambda e, half=half: e.tensor_copy(
                             MC[0:1, half * 512:(half + 1) * 512],
                             psX[0:1, 0:512]))
                    cs_cast[0] = cnt["dve"]
                vmark = cnt["dve"]
                step("sync", [("dve", vmark)], "cs0",
                     lambda e, pi=pi: e.dma_start(
                         out=corr_scr[2 * pi + 1:2 * pi + 2, :], in_=MC))
                mc_war[0] = ("cs0", cnt["cs0"])

            for pi in range(4):
                bias_row(pi)

            casts(0, ("a", 0))
            for c in range(NCT):
                stat_pair(0, c)
            ln_rows(0)
            x_dma(1, ("a", 1))
            colsum_1024(0, 0, "wq", "cbq")
            for c in range(NCT):
                w = []
                if c == 0:
                    w = [("pool", vr_done), ("dve", cs_cast[0])]
                mm(w, psX[0:32, 0:1], vr_sb[:, c * 32:(c + 1) * 32],
                   cst_sb[:, 128:129], c == 0, c == 7)
            pemark = cnt["pe"]
            step("vector", [("pe", pemark)], "dve",
                 lambda e: e.tensor_copy(p_sb[0:32, 32:33], psX[0:32, 0:1]))
            cs_cast[0] = cnt["dve"]
            vrc_mark = cnt["dve"]

            for i in range(4):
                nc_ = (i + 1, ("a", i + 1)) if i + 1 <= 3 else None
                ns_ = i + 1 if i + 1 <= 3 else None
                qproj(i, nc_, ns_)
                for u in range(2):
                    w_unit(wf_t, 8192, 2 * i + u, "wf")
                if i + 2 <= 3:
                    x_dma(i + 2, ("a", i + 2))

            colsum_1024(3, 8192, "wf", "cbg", extra=p_sb[0:32, 32:33],
                        extra_w=[("dve", vrc_mark), ("pool", ct_done)])
            back3(2)
            x_dma(0, ("g", 0))
            x_dma(1, ("g", 1))
            gproj(2, None, None, bk_next=3)
            w_unit(wk_t, 0, 0, "wk")
            w_unit(wk_t, 0, 1, "wk")
            gproj(3, None, None, bk_next=0, precast=(0, ("g", 0)))
            x_dma(4, ("a", 4))
            w_unit(wk_t, 0, 2, "wk")
            w_unit(wk_t, 0, 3, "wk")
            gproj(0, None, None, bk_next=1, precast=(1, ("g", 1)))
            w_unit(wk_t, 0, 4, "wk")
            w_unit(wk_t, 0, 5, "wk")
            gproj(1, None, 4, precast=(4, ("a", 4)))
            w_unit(wk_t, 0, 6, "wk")
            w_unit(wk_t, 0, 7, "wk")
            p1_spill = [("qe0", cnt["qe0"]), ("qe1", cnt["qe1"]),
                        ("qe2", cnt["qe2"]), ("ge0", cnt["ge0"]),
                        ("ge1", cnt["ge1"]), ("ge2", cnt["ge2"])]

            colsum_1024(1, 0, "wk", "cbk")
            for mi in range(2, 8):
                w = (p1_spill + [("dve", cnt["dve"])]) if mi == 2 else []
                step("sync", w, "msk",
                     lambda e, mi=mi: e.dma_start(out=msk_ap(mi),
                                                  in_=msk_i[mi, :, :]))
            x_dma(5, ("a", 5))
            for i in range(4, 12):
                if i + 2 <= 11:
                    x_dma(i + 2, ("a", i + 2))
                nc_ = (i + 1, ("a", i + 1)) if i + 1 <= 11 else None
                ns_ = i + 1 if i + 1 <= 11 else None
                kproj(i, nc_, ns_)
                if i < 8:
                    for u in range(2):
                        w_unit(wv_t, 8192, 2 * (i - 4) + u, "wv")
                if i == 9:
                    colsum_1024(2, 8192, "wv", "cbv")

            back_mu(10)
            Vorder = [10, 11, 4, 5, 6, 7, 8, 9]
            x_dma(4, ("v", 4))
            for j, tcv in enumerate(Vorder):
                if 3 <= j + 2 < 8:
                    x_dma(Vorder[j + 2], ("v", Vorder[j + 2]))
                nc_ = (Vorder[j + 1], ("v", Vorder[j + 1])) \
                    if 2 <= j + 1 < 8 else None
                vproj(tcv, nc_,
                      bk_next=(Vorder[j + 1] if j + 1 < 8 else None))
                if tcv == 7:
                    for mi in range(2):
                        w = [cast_mark[(8, 7)], cast_mark[(8, 6)]] \
                            if mi == 0 else []
                        step("sync", w, "msk",
                             lambda e, mi=mi: e.dma_start(
                                 out=msk_ap(mi), in_=msk_i[mi, :, :]))
                w_unit(wo_t, 0, j, "wo")
            p3_pe = cnt["pe"]
            p3_dve = cnt["dve"]
            p3_spill = p1_spill + [("ve0", cnt["ve0"]), ("ve1", cnt["ve1"])]
            wo_mark = wload[("wo", 7)]
            lb_last = lbm[11]

            gp_war = [0, 0]
            xo_war = [0, 0]
            op_war = [0, 0]
            gs_war = [0, 0]
            gmarks = {}
            xmarks = {}
            tiles = [(tc, ft) for tc in range(NTC_OWN) for ft in range(NP)]

            def xo_ap(sl):
                if sl == 0:
                    return xo_sb[:, 0:512]
                return ar2[:, 3072:3584]

            def gp_ap(sl):
                if sl == 0:
                    return gp_sb[:, 0:512]
                return ar2[:, 0:512]

            def gs_ap(sl):
                if sl == 0:
                    return gs_sb[:, :]
                return ar2[:, 3584:4096]

            def p5_dma(idx):
                tc, ft = tiles[idx]
                pp = idx % 2
                gsm = "gp0" if pp == 0 else "gp1"
                xsm = "xoA" if pp == 0 else "xoB"
                w = list(p1_spill)
                if gp_war[pp]:
                    w.append(("act", gp_war[pp]))
                elif pp == 1:
                    w.append(("dve", p3_dve))
                step("sync", w, gsm,
                     lambda e, ft=ft, tc=tc, pp=pp: e.dma_start(
                         out=gp_ap(pp),
                         in_=g_dram[ft * 128:(ft + 1) * 128,
                                    tc * 512:(tc + 1) * 512]))
                gmarks[idx] = (gsm, cnt[gsm])
                wx = [("dve", xo_war[pp])] if xo_war[pp] else []
                step("sync", wx, xsm,
                     lambda e, ft=ft, tc=tc, pp=pp: e.dma_start(
                         out=xo_ap(pp),
                         in_=x_o[ft * 128:(ft + 1) * 128,
                                 tc * 512:(tc + 1) * 512]))
                xmarks[idx] = (xsm, cnt[xsm])

            mskmark = cnt["msk"]

            qld_war = [0, 0]
            vld_war = [0, 0]
            qrdy = [0, 0]
            vrdy = [0, 0]

            def qvload(p):
                s = p % 2
                w = (list(p3_spill) + [("dve", p3_dve)]) if p <= 1 else (
                    [("pe", qld_war[s])] if qld_war[s] else [])
                qsm = "q0" if s == 0 else "q1"
                step("sync", w, qsm,
                     lambda e, p=p, s=s: e.dma_start(
                         out=q_sb[:, s * 2048: s * 2048 + 2048],
                         in_=q_dram[p * 128:(p + 1) * 128, :]))
                qrdy[s] = cnt[qsm]
                w = (list(p3_spill) + [("dve", p3_dve)]) if p <= 1 else (
                    [("pe", vld_war[s])] if vld_war[s] else [])

                def vload(e, p=p, s=s):
                    sap = bass.AP(vp_sb, s * NKT * 130,
                                  [[2 * NKT * 130, 128], [130, NKT], [1, 130]])
                    dap = bass.AP(v_dram, p * NKT * 128 * 130,
                                  [[130, 128], [128 * 130, NKT], [1, 130]])
                    return e.dma_start(out=sap, in_=dap)
                vsm = "v0" if s == 0 else "v1"
                step("sync", w, vsm, vload)
                vrdy[s] = cnt[vsm]

            sc_free = [0, 0]
            pslot_free = [0, 0, 0]
            onorm_war = [0, 0]
            pending_norm = []
            pend_av = []
            pending_muls = []
            nrm_idx = [0]
            last_muls = [0]
            scrA = [0, 0]
            scrB = [0, 0]
            opr = [[psO[0], psO[1]], [psB, psX]]

            def do_norm(p, m, opair, avdone):
                pn = nrm_idx[0] % 2
                first = nrm_idx[0] == 0
                nrm_idx[0] += 1
                sA = "nb0" if pn == 0 else "nb1"
                sB = "nc0" if pn == 0 else "nc1"
                w4 = [("pe", avdone)]
                if scrA[1 - pn]:
                    w4.append(("nb1" if pn == 0 else "nb0", scrA[1 - pn]))
                if first:
                    w4.append(("dve", p3_dve))
                    w4.append(lb_last)
                step("scalar", w4, "act",
                     lambda e, op0=opair[0]: e.activation(
                         r_m, op0[64:65, 0:512], AF.Ln))
                step("scalar", [], "act",
                     lambda e: e.activation(r_e, r_m, AF.Exp, scale=-1.0))
                step("scalar", [], "act",
                     lambda e, op1=opair[1]: e.activation(
                         r_m, op1[64:65, 0:512], AF.Ln))
                step("scalar", [], "act",
                     lambda e: e.activation(r_t, r_m, AF.Exp, scale=-1.0))
                amark = cnt["act"]
                wb = [("act", amark)]
                if scrB[pn]:
                    wb.append((sB, scrB[pn]))
                step("sync", wb, sA,
                     lambda e, pn=pn: e.dma_start(
                         out=nrm_scr[pn:pn + 1, :], in_=ar2[0:1, 2048:3072]))
                scrA[pn] = cnt[sA]
                wc = [(sA, scrA[pn])]
                if last_muls[0]:
                    wc.append(("dve", last_muls[0]))
                elif first:
                    wc.append(("dve", p3_dve))
                    wc.append(lb_last)
                step("sync", wc, sB,
                     lambda e, pn=pn: e.dma_start(
                         out=ar2[0:64, 1024:1536],
                         in_=bass.AP(nrm_scr, pn * 1024, [[0, 64], [1, 512]])))
                step("sync", [], sB,
                     lambda e, pn=pn: e.dma_start(
                         out=ar2[64:128, 1024:1536],
                         in_=bass.AP(nrm_scr, pn * 1024 + 512,
                                     [[0, 64], [1, 512]])))
                scrB[pn] = cnt[sB]
                bmark = scrB[pn]

                def norm_muls(p=p, m=m, opair=opair, sB=sB, bmark=bmark):
                    step("vector", [(sB, bmark)], "dve",
                         lambda e, p=p, m=m, op0=opair[0]: e.tensor_mul(
                             o_half_ap(p, m, 0), op0[0:64, 0:512], bc_l_lo))
                    step("vector", [], "dve",
                         lambda e, p=p, m=m, op1=opair[1]: e.tensor_mul(
                             o_half_ap(p, m, 1), op1[0:64, 0:512], bc_l_hi))
                    last_muls[0] = cnt["dve"]
                    onorm_war[m % 2] = cnt["dve"]
                pending_muls.append(norm_muls)

            qvload(0)
            for p in range(NP):
                s = p % 2
                for m in range(NTC_OWN):
                    L = 8 * m + 8
                    opair = opr[m % 2]
                    pd = {}

                    def psl(r):
                        if r < 2:
                            return p_sb[:, r * 1024: r * 1024 + 1024]
                        return op_bf[:, 0:1024]

                    for kt in range(L + 1):
                        if kt == 4 and pending_norm:
                            do_norm(*pending_norm.pop())
                        if kt == min(12, L) and pending_muls:
                            pending_muls.pop()()
                        if kt < L:
                            rot = kt % 2
                            r3 = kt % 3
                            for h in range(2):
                                w = []
                                if h == 0:
                                    if kt == 0:
                                        w.append(("q0" if s == 0 else "q1",
                                                  qrdy[s]))
                                        if p == 0 and m == 0:
                                            w.append(("dve", p3_dve))
                                    if sc_free[rot]:
                                        w.append(("act", sc_free[rot]))
                                mm(w, ps[2 * rot + h][:, :],
                                   k_sb[64 * h:64 * h + 64,
                                        p * T + kt * 128: p * T + kt * 128 + 128],
                                   q_sb[64 * h:64 * h + 64,
                                        s * 2048 + m * 512:
                                        s * 2048 + m * 512 + 512],
                                   True, True)
                            if kt == 0 and pend_av:
                                pend_av.pop()()
                            if m == 0 and kt == 2 and p + 1 < NP:
                                qvload(p + 1)
                            if p == 7 and m == 0 and kt == 3:
                                p5_dma(0)
                                p5_dma(1)
                            smark = cnt["pe"]
                            wv = [("pe", smark)]
                            if pslot_free[r3]:
                                wv.append(("pe", pslot_free[r3]))
                            if p == 0 and m == 0 and kt == 0:
                                wv.append(lb_last)
                            step("scalar", wv, "act",
                                 lambda e, r3=r3, rot=rot: e.activation(
                                     psl(r3), psS[rot][:, :], AF.Exp,
                                     bias=-4.0, scale=0.125))
                            sc_free[rot] = cnt["act"]
                            dep = ("act", cnt["act"])
                            if kt >= 8 * m:
                                i = kt - 8 * m
                                for h in range(2):
                                    step("vector",
                                         [dep, ("msk", mskmark)], "dve",
                                         lambda e, r3=r3, h=h, i=i:
                                         e.tensor_mul(
                                             psl(r3)[:, h * 512:
                                                     h * 512 + 512],
                                             psl(r3)[:, h * 512:
                                                     h * 512 + 512],
                                             msk_ap(i)))
                                dep = ("dve", cnt["dve"])
                            pd[kt] = dep
                        if 2 <= kt and kt - 2 <= L - 2:
                            ka = kt - 2
                            rka = ka % 3
                            dep = pd.pop(ka)
                            for h in range(2):
                                w2 = [dep] if h == 0 else []
                                if ka == 0:
                                    if onorm_war[m % 2]:
                                        w2.append(("dve", onorm_war[m % 2]))
                                    if h == 0:
                                        w2.append(("v0" if s == 0 else "v1",
                                                   vrdy[s]))
                                mm(w2, opair[h][0:65, 0:512],
                                   vp_sb[:, s * NKT * 130 + ka * 130 + h * 65:
                                         s * NKT * 130 + ka * 130 + h * 65 + 65],
                                   psl(rka)[:, h * 512: h * 512 + 512],
                                   ka == 0, False)
                            pslot_free[rka] = cnt["pe"]
                        if kt == L:
                            dep = pd.pop(L - 1)

                            def av_tail(p=p, m=m, s=s, L=L, opair=opair,
                                        dep=dep, psl=psl):
                                rka = (L - 1) % 3
                                for h in range(2):
                                    w2 = [dep] if h == 0 else []
                                    mm(w2, opair[h][0:65, 0:512],
                                       vp_sb[:, s * NKT * 130 + (L - 1) * 130
                                             + h * 65:
                                             s * NKT * 130 + (L - 1) * 130
                                             + h * 65 + 65],
                                       psl(rka)[:, h * 512: h * 512 + 512],
                                       False, True)
                                pslot_free[rka] = cnt["pe"]
                                qld_war[s] = cnt["pe"]
                                vld_war[s] = cnt["pe"]
                                pending_norm.append((p, m, opair, cnt["pe"]))
                            pend_av.append(av_tail)
                qld_war[s] = cnt["pe"]
                vld_war[s] = cnt["pe"]
            if pend_av:
                pend_av.pop()()
            if pending_norm:
                do_norm(*pending_norm.pop())
            while pending_muls:
                pending_muls.pop(0)()
            p4_pe = cnt["pe"]

            projp_war = {4: 0, 5: 0, 6: 0}
            for idx, (tc, ft) in enumerate(tiles):
                pp = idx % 2
                bk5 = 4 + idx % 3
                for cp in range(NCT):
                    w2 = []
                    if cp == 0:
                        if projp_war[bk5]:
                            w2 = [("dve", projp_war[bk5])]
                        else:
                            w2 = [("dve", last_muls[0]),
                                  ("pool", wo_mark)]
                    mm(w2, ps[bk5][:, :],
                       w_sb[:, cp * 1024 + ft * 128:
                            cp * 1024 + ft * 128 + 128],
                       o_rhs_ap(cp, tc), cp == 0, cp == 7)
                pemark = cnt["pe"]
                wa = [gmarks[idx]]
                if gs_war[pp]:
                    wa.append(("dve", gs_war[pp]))
                step("scalar", wa, "act",
                     lambda e, pp=pp: e.activation(
                         gs_ap(pp), gp_ap(pp), AF.Silu))
                gp_war[pp] = cnt["act"]
                amark = cnt["act"]
                wv = [("pe", pemark), ("act", amark), xmarks[idx]]
                if op_war[pp]:
                    wv.append(("op0" if pp == 0 else "op1", op_war[pp]))
                step("vector", wv, "dve",
                     lambda e, ft=ft, bk5=bk5: e.tensor_scalar(
                         out=bc_l, in0=ps[bk5][:, :],
                         scalar1=bo_sb[:, ft:ft + 1], scalar2=None,
                         op0=ALU.add))
                projp_war[bk5] = cnt["dve"]
                step("vector", [], "dve",
                     lambda e, pp=pp: e.tensor_mul(bc_l, bc_l, gs_ap(pp)))
                gs_war[pp] = cnt["dve"]
                step("vector", [], "dve",
                     lambda e, pp=pp: e.tensor_add(
                         op_sb[:, pp * 512: pp * 512 + 512], bc_l,
                         xo_ap(pp)))
                xo_war[pp] = cnt["dve"]
                vmark = cnt["dve"]
                osm = "op0" if pp == 0 else "op1"
                step("gpsimd", [("dve", vmark)], osm,
                     lambda e, ft=ft, tc=tc, pp=pp: e.dma_start(
                         out=out_t[ft * 128:(ft + 1) * 128,
                                   tc * 512:(tc + 1) * 512],
                         in_=op_sb[:, pp * 512: pp * 512 + 512]))
                op_war[pp] = cnt[osm]
                if idx + 2 < len(tiles):
                    p5_dma(idx + 2)

            if role == "gpsimd":
                for sname in DMA_SEMS:
                    if cnt[sname]:
                        eng.wait_ge(sems[sname], cnt[sname])

        with nc.Block() as block:
            block.tensor(lambda e: plan("tensor", e))
            block.scalar(lambda e: plan("scalar", e))
            block.vector(lambda e: plan("vector", e))
            block.sync(lambda e: plan("sync", e))
            block.gpsimd(lambda e: plan("gpsimd", e))

    return nc


def make_inputs(x, qkv_w, qkv_b, out_w, out_b, W, b, V_r, Cm):
    bf16 = ml_dtypes.bfloat16
    tri = np.zeros((4, 128, 512), dtype=np.float32)
    for r in range(4):
        kk = np.arange(128)[:, None]
        qq = np.arange(512)[None, :]
        tri[r] = (qq >= kk + 128 * r).astype(np.float32)
    ones_t = np.ones((128, 512), dtype=np.float32)
    zeros_t = np.zeros((128, 512), dtype=np.float32)
    cst = np.concatenate([np.ones((128, 128), np.float32),
                          -np.ones((128, 128), np.float32),
                          np.eye(128, dtype=np.float32)], axis=1)
    cv = np.tile(np.array([[-4.0, 0.125, 1e-5, -0.5, -1.0, 1.0, 0.5, 0.0]],
                          np.float32), (128, 1))
    wq_t = np.ascontiguousarray(qkv_w[0:1024].T)
    wk_t = np.ascontiguousarray(qkv_w[1024:2048].T)
    wv_t = np.ascontiguousarray(qkv_w[2048:3072].T)
    wo_t = np.ascontiguousarray(out_w.T)
    wf_t = np.ascontiguousarray(W.T)
    ct_t = np.ascontiguousarray(Cm.T)
    bq = np.ascontiguousarray(qkv_b[0:1024].reshape(8, 128).T)
    bk = np.ascontiguousarray(qkv_b[1024:2048].reshape(8, 128).T)
    bv = np.ascontiguousarray(qkv_b[2048:3072].reshape(8, 128).T)
    bo = np.ascontiguousarray(out_b.reshape(8, 128).T)
    bf = np.ascontiguousarray(b.reshape(8, 128).T)
    in_maps = []
    for core in range(8):
        bb, j = core // 2, core % 2
        xt = np.ascontiguousarray(x[bb].T)
        own = np.concatenate(
            [xt[:, (2 * m + j) * 512:(2 * m + j + 1) * 512] for m in range(4)],
            axis=1)
        if j == 0:
            msk = np.stack([tri[0], tri[1], tri[2], tri[3],
                            zeros_t, zeros_t, zeros_t, zeros_t])
        else:
            msk = np.stack([ones_t, ones_t, ones_t, ones_t,
                            tri[0], tri[1], tri[2], tri[3]])
        in_maps.append({
            "x_t": xt, "x_o": np.ascontiguousarray(own),
            "wq_t": wq_t, "wk_t": wk_t, "wv_t": wv_t, "wo_t": wo_t,
            "wf_t": wf_t, "vr_i": np.ascontiguousarray(V_r),
            "ct_i": ct_t, "bq_i": bq, "bk_i": bk, "bo_i": bo, "bf_i": bf,
            "bv_i": bv, "msk_i": msk.astype(bf16),
            "cst_i": cst.astype(bf16), "cv_i": cv,
        })
    return in_maps


def kernel(L0_attn_x, L0_attn_qkv_w, L0_attn_qkv_b, L0_attn_out_w,
           L0_attn_out_b, L0_ffn_W, L0_ffn_b, L0_ffn_V_r, L0_ffn_C):
    x = np.asarray(L0_attn_x, dtype=np.float32)
    in_maps = make_inputs(x,
                          np.asarray(L0_attn_qkv_w, np.float32),
                          np.asarray(L0_attn_qkv_b, np.float32),
                          np.asarray(L0_attn_out_w, np.float32),
                          np.asarray(L0_attn_out_b, np.float32),
                          np.asarray(L0_ffn_W, np.float32),
                          np.asarray(L0_ffn_b, np.float32),
                          np.asarray(L0_ffn_V_r, np.float32),
                          np.asarray(L0_ffn_C, np.float32))
    nc = build_nc()
    res = run_bass_kernel_spmd(nc, in_maps, core_ids=list(range(8)))
    out = np.empty((4, 4096, 1024), dtype=np.float32)
    for bb in range(4):
        obT = np.empty((1024, 4096), dtype=np.float32)
        for j in range(2):
            o = np.asarray(res.results[2 * bb + j]["out_t"])
            for m in range(4):
                obT[:, (2 * m + j) * 512:(2 * m + j + 1) * 512] = \
                    o[:, m * 512:(m + 1) * 512]
        out[bb] = obT.T
    return out


if __name__ == "__main__":
    nc = build_nc()
    print("build ok")
